# version 07
# baseline (speedup 1.0000x reference)
import numpy as np
import ml_dtypes
from contextlib import ExitStack

import concourse.bass as bass
import concourse.mybir as mybir
from concourse.bass_utils import run_bass_kernel_spmd

F32 = mybir.dt.float32
BF16 = mybir.dt.bfloat16
I32 = mybir.dt.int32
U16 = mybir.dt.uint16
AF = mybir.ActivationFunctionType
ALU = mybir.AluOpType
AX = mybir.AxisListType
BF = ml_dtypes.bfloat16

NCORE = 8
DEBUG = True
NO_CC = False
SERIAL_DMA = True
CC_LIMIT = 99
DQG = 16
DQMODE = "full"
ATTN_NH = 16
ATTN_PARTS = "all"
NG3 = 8
A3CHAIN = True
A3DEP = True
DECL = 0xFFFF
EXT_NAMES = set()
STAGE_DMA_Q = "scalar"
UPTO = "Z"
D = 2048
TC = 1024
NFT = D // 128
EPS_LN = 1e-5
EPS_A1 = 1e-15
ATT_SC = 1.0 / float(np.sqrt(128.0))

GELU_ADD = 1.0 / 0.044715
GELU_SCALE = 2.0 * 0.7978845608028654 * 0.044715


class Prog:
    ENGINES = ("sync", "vector", "scalar", "tensor", "gpsimd")

    def __init__(self, nc):
        self.nc = nc
        self.ops = []
        self._bar = []
        self._cc_ops = []
        self._last_dma = {}

    def add(self, eng, fn, deps=(), dma=False, se=False, nobar=False, cc=False):
        deps = [d for d in deps if d is not None]
        if not nobar:
            deps = deps + self._bar
        if dma and SERIAL_DMA and eng in self._last_dma:
            deps = deps + [self._last_dma[eng]]
        self.ops.append((eng, fn, deps, dma, se or cc))
        if dma:
            self._last_dma[eng] = len(self.ops) - 1
        if cc:
            self._cc_ops.append(len(self.ops) - 1)
        return len(self.ops) - 1

    def barrier(self):
        last = {}
        for i, (eng, *_ ) in enumerate(self.ops):
            last[eng] = i
        self._bar = list(last.values())

    def build(self, stack):
        nc = self.nc
        sems = {}
        for e in self.ENGINES:
            sems[(e, "c")] = stack.enter_context(nc.semaphore("semc_" + e))
            sems[(e, "d")] = stack.enter_context(nc.semaphore("semd_" + e))
        SELF_SYNC = ("vector", "scalar", "gpsimd")
        needed = set()
        for eng, fn, deps, dma, se in self.ops:
            for d in deps:
                peng, _, _, pdma, pse = self.ops[d]
                if peng != eng or pse or pdma or peng in SELF_SYNC:
                    needed.add(d)
        last_of = {}
        for i, (eng, fn, deps, dma, se) in enumerate(self.ops):
            last_of[(eng, dma)] = i
            if dma:
                needed.add(i)
        needed.update(self._cc_ops)
        needed.update(last_of.values())
        val = {}
        ctr = {(e, k): 0 for e in self.ENGINES for k in ("c", "d")}
        kind = {}
        for i, (eng, fn, deps, dma, se) in enumerate(self.ops):
            kind[i] = (eng, "d" if dma else "c")
            if i in needed:
                ctr[kind[i]] += 16 if dma else 1
                val[i] = ctr[kind[i]]
        block = stack.enter_context(nc.Block())

        def make_body(ename):
            def body(eng):
                waited = {}
                for i, (oeng, fn, deps, dma, se) in enumerate(self.ops):
                    if oeng != ename:
                        continue
                    for d in deps:
                        peng, _, _, pdma, pse = self.ops[d]
                        if peng == ename and not (pse or pdma) and \
                                ename not in SELF_SYNC:
                            continue
                        v = val.get(d)
                        k = kind[d]
                        if v is None or waited.get(k, 0) >= v:
                            continue
                        eng.wait_ge(sems[k], v)
                        waited[k] = v
                    inst = fn(eng)
                    if i in needed:
                        inst.then_inc(sems[kind[i]], 16 if dma else 1)
                if ename == "gpsimd":
                    for k2, c2 in ctr.items():
                        if c2:
                            eng.wait_ge(sems[k2], c2)
            return body

        for e in self.ENGINES:
            getattr(block, e)(make_body(e))
        return sems, ctr


class Scratch:

    def __init__(self, nc, backing_handle):
        self.nc = nc
        self.backing = backing_handle
        self.base = nc.lookup_mloc(backing_handle).addr
        self.cap = backing_handle.shape[1]
        self.off = 0
        self.n = 0

    def reset(self, base=0):
        self.off = base

    def tile(self, name, shape, dtype):
        per_part = int(np.prod(shape[1:])) * mybir.dt.size(dtype)
        off = (self.off + 31) & ~31
        assert off + per_part <= self.cap, (
            f"scratch overflow: {name} needs {per_part}B at {off}, cap {self.cap}")
        self.off = off + per_part
        self.n += 1
        nm = f"sc{self.n}_{name}"
        hnd = self.nc.alloc_sbuf_tensor_at(nm, shape, dtype, offset=self.base + off)
        return hnd[:]



def emit_ln(p, B, x_dep, extra_rdeps=(), psum_war=()):
    nc = p.nc
    xt, lnx = B["xt"], B["lnx"]
    xb, x2b = B["xb"], B["x2b"]
    onescol = B["onescol"]
    onesrow_f = B["onesrow_f"]
    srow = B["srow"]
    rstd = B["rstd"]
    ps_a = B["ps_a"]
    ps_b = B["ps_b"]
    sum_ps = ps_b[0:1, 0:TC]
    sq_ps = ps_b[32:33, 0:TC]
    rstd_b = ps_a[:, 0:TC]
    mr_b = ps_a[:, TC:2 * TC]
    m_row = srow[:, 0:TC]
    q_row = srow[:, TC:2 * TC]
    var_row = srow[:, 2 * TC:3 * TC]
    mr_row = srow[:, 3 * TC:4 * TC]

    cast_ops = [None, None]
    sq_ops = [None, None]
    mm_s_ops = [None, None]
    mm_q_ops = [None, None]
    o_sum = o_sq = None
    first_deps = list(psum_war)
    for ft in range(NFT):
        r = ft % 2
        c = p.add("vector", lambda v, ft=ft, r=r: v.tensor_copy(out=xb[:, r], in_=xt[:, ft]),
                  deps=list(x_dep[ft]) + [mm_s_ops[r]])
        s = p.add("scalar", lambda s, ft=ft, r=r: s.activation(out=x2b[:, r], in_=xt[:, ft],
                                                               func=AF.Square),
                  deps=list(x_dep[ft]) + [mm_q_ops[r]])

        def _mm_half(t, out_ps, rhs, ft):
            for h in (0, 1):
                inst = t.matmul(out=out_ps[:, h * 512:(h + 1) * 512], lhsT=onescol,
                                rhs=rhs[:, h * 512:(h + 1) * 512],
                                start=(ft == 0), stop=(ft == NFT - 1))
            return inst
        o_sum = p.add("tensor", lambda t, ft=ft, r=r: _mm_half(t, sum_ps, xb[:, r], ft),
                      deps=[c] + list(extra_rdeps) + (first_deps if ft == 0 else []))
        o_sq = p.add("tensor", lambda t, ft=ft, r=r: _mm_half(t, sq_ps, x2b[:, r], ft),
                     deps=[s] + list(extra_rdeps) + (first_deps if ft == 0 else []))
        cast_ops[r] = c
        sq_ops[r] = s
        mm_s_ops[r] = o_sum
        mm_q_ops[r] = o_sq

    o_m = p.add("vector", lambda v: v.tensor_single_scalar(out=m_row, in_=sum_ps,
                                                           scalar=1.0 / D, op=ALU.mult),
                deps=[o_sum], se=True)
    o_q = p.add("vector", lambda v: v.tensor_single_scalar(out=q_row, in_=sq_ps,
                                                           scalar=1.0 / D, op=ALU.mult),
                deps=[o_sq], se=True)
    o_m2 = p.add("vector", lambda v: v.scalar_tensor_tensor(out=var_row, in0=m_row, scalar=-1.0,
                                                            in1=m_row, op0=ALU.mult, op1=ALU.mult),
                 deps=[o_m], se=True)
    o_var = p.add("vector", lambda v: v.scalar_tensor_tensor(
        out=var_row, in0=q_row, scalar=EPS_LN, in1=var_row, op0=ALU.add, op1=ALU.add),
        deps=[o_q, o_m2], se=True)
    o_ln = p.add("scalar", lambda s: s.activation(out=var_row, in_=var_row, func=AF.Ln),
                 deps=[o_var], se=True)
    o_rstd = p.add("scalar", lambda s: s.activation(out=rstd, in_=var_row, func=AF.Exp,
                                                    scale=-0.5),
                   deps=[o_ln], se=True)
    o_mr = p.add("vector", lambda v: v.tensor_tensor(out=mr_row, in0=m_row, in1=rstd,
                                                     op=ALU.mult),
                 deps=[o_m, o_rstd], se=True)

    def _bcast(t, out_ps, row):
        for h in (0, 1):
            inst = t.matmul(out=out_ps[:, h * 512:(h + 1) * 512], lhsT=onesrow_f,
                            rhs=row[:, h * 512:(h + 1) * 512], start=True, stop=True)
        return inst
    o_rb = p.add("tensor", lambda t: _bcast(t, rstd_b, rstd),
                 deps=[o_rstd] + list(extra_rdeps) + list(psum_war))
    o_mb = p.add("tensor", lambda t: _bcast(t, mr_b, mr_row),
                 deps=[o_mr] + list(extra_rdeps) + list(psum_war))
    ln_ops = []
    tmp = B["lntmp"]
    for ft in range(NFT):
        o_t1 = p.add("vector", lambda v, ft=ft: v.tensor_tensor(out=tmp, in0=xt[:, ft],
                                                                in1=rstd_b, op=ALU.mult),
                     deps=[o_rb] + list(x_dep[ft]))
        o_t2 = p.add("vector", lambda v, ft=ft: v.tensor_tensor(out=lnx[:, ft], in0=tmp,
                                                                in1=mr_b, op=ALU.subtract),
                     deps=[o_mb, o_t1])
        ln_ops.append(o_t2)
    return ln_ops, ln_ops + [o_m, o_q]


def emit_bias_cast(p, tmp_f, dst_b, b_dram, width, scale=None, after=()):
    last = None
    prev_cast = list(after)
    for ch in range(width // 512):
        d = p.add("sync", lambda e, ch=ch: e.dma_start(
            out=tmp_f, in_=b_dram[:, ch * 512:(ch + 1) * 512]), dma=True, deps=prev_cast)
        if scale is None:
            last = p.add("vector", lambda v, ch=ch: v.tensor_copy(
                out=dst_b[:, ch * 512:(ch + 1) * 512], in_=tmp_f), deps=[d], se=True)
        else:
            last = p.add("vector", lambda v, ch=ch: v.tensor_single_scalar(
                out=dst_b[:, ch * 512:(ch + 1) * 512], in_=tmp_f, scalar=scale,
                op=ALU.mult), deps=[d], se=True)
        prev_cast = [last]
    return last



def build_mega():
    nc = bass.Bass("TRN2", target_bir_lowering=False, debug=False)
    EXT_NAMES.clear()
    EXT_NAMES.update(("x_in", "g_pk", "g_sc"))

    def ext_in(name, shape, dtype):
        EXT_NAMES.add(name)
        return nc.dram_tensor(name, shape, dtype, kind="ExternalInput").ap()

    def internal(name, shape, dtype):
        return nc.dram_tensor(name, shape, dtype, kind="Internal").ap()

    x_in = nc.dram_tensor("x_in", [D, TC], F32, kind="ExternalInput").ap()
    g_pk = nc.dram_tensor("g_pk", [16384, 128], I32, kind="ExternalInput").ap()
    g_sc = nc.dram_tensor("g_sc", [16384, 8], F32, kind="ExternalInput").ap()
    if DECL & 1:
        wq = ext_in("wq", [D, D], F32)
        wk = ext_in("wk", [D, D], F32)
        wv = ext_in("wv", [D, D], F32)
        wo = ext_in("wo", [D, D], F32)
        bq = ext_in("bq", [1, D], F32)
        bk = ext_in("bk", [1, D], F32)
        bv = ext_in("bv", [1, D], F32)
        bo = ext_in("bo", [1, D], F32)
    if DECL & 2:
        u_pk = ext_in("u_pk", [16384, 128], I32)
        u_sc = ext_in("u_sc", [16384, 8], F32)
        d_pk = ext_in("d_pk", [2048, 1024], I32)
        d_sc = ext_in("d_sc", [2048, 64], F32)
    if DECL & 4:
        w1_s = ext_in("w1_s", [4096, 256], F32)
        w2_s = ext_in("w2_s", [4096, 512], F32)
        w3_s = ext_in("w3_s", [2048, 512], F32)
        qkv_s = ext_in("qkv_s", [6144, 256], F32)
        proj_s = ext_in("proj_s", [2048, 256], F32)
        b1 = ext_in("b1", [1, 4096], F32)
        b2 = ext_in("b2", [1, 4096], F32)
        b3 = ext_in("b3", [1, D], F32)
        bproj = ext_in("bproj", [1, D], F32)
    if DECL & 8:
        ident = ext_in("ident", [128, 128], BF16)
        maskA_in = ext_in("maskA", [128, 128], BF16)
        maskB_in = ext_in("maskB", [128, 128], BF16)
        maskC_in = ext_in("maskC", [128, 1024], BF16)
        bones = ext_in("bones", [4, 128], BF16)
    x_out = nc.dram_tensor("x_out", [D, TC], F32, kind="ExternalOutput").ap()

    gT_s = internal("gT_s", [256, 16384], BF16)
    gT = internal("gT", [D, 16384], BF16)
    uT = internal("uT", [D, 16384], BF16)
    if DECL & 16:
        uT_s = internal("uT_s", [256, 16384], BF16)
        dT_s = internal("dT_s", [2048, 2048], BF16)
        dT = internal("dT", [16384, D], BF16)
        w1T_s = internal("w1T_s", [256, 4096], BF16)
        w2T_s = internal("w2T_s", [512, 4096], BF16)
        w3T_s = internal("w3T_s", [512, 2048], BF16)
        qkvT_s = internal("qkvT_s", [256, 6144], BF16)
        projT_s = internal("projT_s", [256, 2048], BF16)
        w1T = internal("w1T", [D, 4096], BF16)
        w2T = internal("w2T", [4096, 4096], BF16)
        w3T = internal("w3T", [4096, D], BF16)
        qkvT = internal("qkvT", [D, 6144], BF16)
        projT = internal("projT", [D, D], BF16)
        q0d = internal("q0d", [D, TC], BF16)
        k0d = internal("k0d", [D, TC], BF16)
        v0d = internal("v0d", [D, TC], BF16)
        k0g = internal("k0g", [2 * D, TC], BF16)
        v0g = internal("v0g", [2 * D, TC], BF16)
        q1d = internal("q1d", [D, TC], BF16)
        k1d = internal("k1d", [TC, D], BF16)
        v1d = internal("v1d", [TC, D], BF16)
        vkp = internal("vkp", [32, 2112], F32)
        vkr = internal("vkr", [32, 2112], F32)

    DBG = {}
    if DEBUG:
        EXT_NAMES.add("dbg_dts"); EXT_NAMES.add("dbg_dtg")
        DBG["dbg_dts"] = nc.dram_tensor("dbg_dts", [2048, 2048], BF16,
                                        kind="ExternalOutput").ap()
        DBG["dbg_dtg"] = nc.dram_tensor("dbg_dtg", [2048, 2048], BF16,
                                        kind="ExternalOutput").ap()
    if DEBUG and (DECL & 32):
        for tag in ("dbg_f0a", "dbg_p0", "dbg_f0b", "dbg_f1a", "dbg_a1"):
            EXT_NAMES.add(tag)
            DBG[tag] = nc.dram_tensor(tag, [D, TC], F32, kind="ExternalOutput").ap()
        EXT_NAMES.add("dbg_ctx")
        DBG["dbg_ctx"] = nc.dram_tensor("dbg_ctx", [D, TC], BF16,
                                        kind="ExternalOutput").ap()

    RG_ALL = [list(range(NCORE))]
    RG_PAIR = [[0, 1], [2, 3], [4, 5], [6, 7]]

    with ExitStack() as stack:
        def sb(name, shape, dtype):
            return stack.enter_context(nc.sbuf_tensor(name, shape, dtype))[:]

        B = {}
        if not (DECL & 64):
            B["xt"] = sb("xt", [128, 16, 8], F32)
        else:
            B["xt"] = sb("xt", [128, NFT, TC], F32)
        if DECL & 64:
            B["lnx"] = sb("lnx", [128, NFT, TC], BF16)
            B["xb"] = sb("xb_", [128, 2, TC], BF16)
            B["x2b"] = sb("x2b", [128, 2, TC], BF16)
            B["onescol"] = sb("onescol", [128, 1], BF16)
            B["onesrow_f"] = sb("onesrow_f", [1, 128], F32)
            B["onesrow_b"] = sb("onesrow_b", [1, TC], BF16)
            B["srow"] = sb("srow", [1, 4 * TC], F32)
            B["rstd"] = sb("rstd", [1, TC], F32)
            B["lntmp"] = sb("lntmp", [128, TC], F32)
            identt = sb("identt", [128, 128], BF16)
            maskA = sb("maskA_t", [128, 128], BF16)
            maskB = sb("maskB_t", [128, 128], BF16)
            maskC = sb("maskC_t", [128, 1024], BF16)
            bonest = sb("bonest", [4, 128], BF16)

        scratch_back = stack.enter_context(nc.sbuf_tensor("scratch", [128, 74 * 1024],
                                                          mybir.dt.uint8))
        SC = Scratch(nc, scratch_back)

        if DECL & 128:
            B["ps_a"] = stack.enter_context(nc.psum_tensor("ps_a", [128, 2048], F32))[:]
            B["ps_b"] = stack.enter_context(nc.psum_tensor("ps_b", [128, 1024], F32))[:]
            ps_t = stack.enter_context(nc.psum_tensor("ps_t", [128, 2048], BF16))[:]
            ps_a, ps_b = B["ps_a"], B["ps_b"]

        p = Prog(nc)

        if UPTO == "A3":
            pk_v = g_pk.rearrange("(g q) w -> q g w", q=128)
            pkt3 = SC.tile("pkt3", [128, NG3, 128], I32)
            prev = []
            for g in range(DQG):
                d1 = p.add("sync", lambda e, g=g: e.dma_start(
                    out=pkt3, in_=pk_v[:, g * NG3:(g + 1) * NG3, :]), dma=True,
                    deps=prev if A3CHAIN else [])
                prev = [d1]
            o1 = p.add("gpsimd", lambda g: g.collective_compute(
                "AllGather", ALU.bypass, replica_groups=RG_ALL,
                ins=[gT_s.opt()], outs=[gT.opt()]), deps=prev if A3DEP else [],
                cc=True)
            o2 = p.add("gpsimd", lambda g: g.collective_compute(
                "AllGather", ALU.bypass, replica_groups=RG_ALL,
                ins=[gT_s.opt()], outs=[uT.opt()]), deps=[o1], cc=True)
            p.add("gpsimd", lambda e: e.dma_start(
                out=x_out[0:128, :],
                in_=pkt3.bitcast(F32).rearrange("p a w -> p (a w)")[:, 0:1024]),
                deps=[o1, o2] + prev, dma=True)
            p.build(stack)
            return nc

        consts = [
            p.add("vector", lambda v: v.memset(B["onescol"], 1.0), se=True),
            p.add("vector", lambda v: v.memset(B["onesrow_f"], 1.0), se=True),
            p.add("vector", lambda v: v.memset(B["onesrow_b"], 1.0), se=True),
        ]
        d_id = p.add("sync", lambda e: e.dma_start(out=identt, in_=ident), dma=True)
        d_mA = p.add("sync", lambda e: e.dma_start(out=maskA, in_=maskA_in), dma=True)
        d_mB = p.add("sync", lambda e: e.dma_start(out=maskB, in_=maskB_in), dma=True)
        d_mC = p.add("sync", lambda e: e.dma_start(out=maskC, in_=maskC_in), dma=True)
        d_bn = p.add("sync", lambda e: e.dma_start(out=bonest, in_=bones), dma=True)
        x_ld = p.add("sync", lambda e: e.dma_start(
            out=B["xt"], in_=x_in.rearrange("(ft p) t -> p ft t", p=128)), dma=True)
        x_dep = {ft: [x_ld] for ft in range(NFT)}

        SC.reset()
        NG = 8
        pkt = SC.tile("pkt", [128, NG, 128], I32)
        sct = SC.tile("sct", [128, NG, 8], F32)
        t_lo = SC.tile("t_lo", [128, NG * 128], I32)
        t_hi = SC.tile("t_hi", [128, NG * 128], I32)
        t_u = SC.tile("t_u", [128, NG * 128], I32)
        t_sg = SC.tile("t_sg", [128, NG * 128], I32)
        t_a = SC.tile("t_a", [128, NG * 128], I32)
        t_b = SC.tile("t_b", [128, NG * 128], I32)
        u16t = SC.tile("u16t", [128, NG * 256], U16)
        dqb = SC.tile("dqb", [128, NG, 256], BF16)
        stgu = SC.tile("stgu", [128, 2, 2048], BF16)
        stg_gu = stgu.rearrange("p s (a n) -> p s a n", a=2)
        stg_dn = stgu.rearrange("p s (a n) -> p s a n", a=16)

        def dequant_gu(pk_dram, sc_dram, out_dram, after=()):
            pk_v = pk_dram.rearrange("(g q) w -> q g w", q=128)
            sc_v = sc_dram.rearrange("(g q) s -> q g s", q=128)
            out_v = out_dram.rearrange("(a q) n -> q a n", q=128)
            dma_dep = list(after)
            st_ops = []
            last_ev = None
            stg_war = [None, None]
            for g in range(DQG):
                sel = g % 2
                d1 = p.add("sync", lambda e, g=g: e.dma_start(
                    out=pkt, in_=pk_v[:, g * NG:(g + 1) * NG, :]), dma=True,
                    deps=dma_dep[-1:])
                d2 = p.add("sync", lambda e, g=g: e.dma_start(
                    out=sct, in_=sc_v[:, g * NG:(g + 1) * NG, :]), dma=True,
                    deps=dma_dep[-1:])
                if DQMODE == "dma":
                    dma_dep = [d1, d2]
                    st_ops.append(d1)
                    continue
                pkf = pkt.rearrange("p a w -> p (a w)")
                o1 = p.add("vector", lambda v: v.tensor_single_scalar(
                    out=t_lo, in_=pkf, scalar=15, op=ALU.bitwise_and), deps=[d1])
                o2 = p.add("vector", lambda v: v.tensor_single_scalar(
                    out=t_hi, in_=pkf, scalar=4, op=ALU.logical_shift_right), deps=[d1])
                uops = []
                for i, plane in enumerate((t_lo, t_hi)):
                    pdep = o1 if i == 0 else o2
                    q1 = p.add("vector", lambda v, plane=plane: v.tensor_single_scalar(
                        out=t_u, in_=plane, scalar=7, op=ALU.bitwise_and), deps=[pdep])
                    q2 = p.add("vector", lambda v, plane=plane: v.tensor_single_scalar(
                        out=t_sg, in_=plane, scalar=8, op=ALU.bitwise_and), deps=[pdep])
                    q3 = p.add("vector", lambda v: v.tensor_scalar(
                        out=t_a, in0=t_u, scalar1=64, scalar2=0x3F00,
                        op0=ALU.mult, op1=ALU.add), deps=[q1])
                    q4 = p.add("vector", lambda v: v.tensor_single_scalar(
                        out=t_b, in_=t_u, scalar=0x3F00, op=ALU.mult), deps=[q1])
                    q5 = p.add("vector", lambda v: v.tensor_tensor(
                        out=t_a, in0=t_a, in1=t_b, op=ALU.min), deps=[q3, q4])
                    q6 = p.add("vector", lambda v: v.scalar_tensor_tensor(
                        out=t_a, in0=t_sg, scalar=4096, in1=t_a,
                        op0=ALU.mult, op1=ALU.add), deps=[q2, q5], se=True)
                    q7 = p.add("vector", lambda v, i=i: v.tensor_copy(
                        out=u16t.rearrange("p (w two) -> p w two", two=2)[:, :, i],
                        in_=t_a), deps=[q6], se=True)
                    uops.append(q7)
                dqv = dqb.rearrange("p a (s e) -> p a s e", e=32)
                o_sc = p.add("vector", lambda v: v.tensor_tensor(
                    out=dqv, in0=u16t.bitcast(BF16).rearrange("p (a s e) -> p a s e",
                                                              a=NG, e=32),
                    in1=sct.unsqueeze(3).broadcast_to((128, NG, 8, 32)),
                    op=ALU.mult), deps=uops + [d2], se=True)
                dma_dep = [o_sc]
                if DQMODE == "vec":
                    st_ops.append(o_sc)
                    continue
                evs = []
                for a in range(NG):
                    tr1 = p.add("tensor", lambda t, a=a: t.transpose(
                        out=ps_t[:, 0:128], in_=dqb[:, a, 0:128], identity=identt),
                        deps=[o_sc, d_id] + ([last_ev] if last_ev is not None else []))
                    tr2 = p.add("tensor", lambda t, a=a: t.transpose(
                        out=ps_t[:, 128:256], in_=dqb[:, a, 128:256], identity=identt),
                        deps=[tr1])
                    ev = p.add("vector", lambda v, a=a, sel=sel: v.tensor_copy(
                        out=stg_gu[:, sel, 0:2, a * 128:(a + 1) * 128],
                        in_=ps_t[:, 0:256].rearrange("p (a n) -> p a n", a=2)),
                        deps=[tr1, tr2, stg_war[sel]])
                    last_ev = ev
                    evs.append(ev)
                st = p.add(STAGE_DMA_Q, lambda e, g=g, sel=sel: e.dma_start(
                    out=out_v[:, :, g * NG * 128:(g + 1) * NG * 128],
                    in_=stg_gu[:, sel, :, :]), deps=evs, dma=True)
                stg_war[sel] = st
                st_ops.append(st)
            return st_ops

        def dequant_down(after=()):
            pk_v = d_pk.rearrange("(g q) w -> q g w", q=128)
            sc_v = d_sc.rearrange("(g q) s -> q g s", q=128)
            out_v = dT_s.rearrange("(a q) n -> q a n", q=128)
            dma_dep = list(after)
            st_ops = []
            last_ev = None
            stg_war = [None, None]
            W = 1024
            for g in range(16):
                sel = g % 2
                d1 = p.add("sync", lambda e, g=g: e.dma_start(
                    out=pkt[:, 0:8, :].rearrange("p a w -> p (a w)"),
                    in_=pk_v[:, g, :]), dma=True, deps=dma_dep[-1:])
                d2 = p.add("sync", lambda e, g=g: e.dma_start(
                    out=sct[:, 0:8, :].rearrange("p a s -> p (a s)"),
                    in_=sc_v[:, g, :]), dma=True, deps=dma_dep[-1:])
                o_half = p.add("vector", lambda v: v.tensor_single_scalar(
                    out=sct[:, 0:8, :], in_=sct[:, 0:8, :], scalar=0.5, op=ALU.mult),
                    deps=[d2], se=True)
                pkf = pkt[:, 0:8, :].rearrange("p a w -> p (a w)")
                o1 = p.add("vector", lambda v: v.tensor_single_scalar(
                    out=t_lo[:, 0:W], in_=pkf, scalar=15, op=ALU.bitwise_and), deps=[d1])
                o2 = p.add("vector", lambda v: v.tensor_single_scalar(
                    out=t_hi[:, 0:W], in_=pkf, scalar=4, op=ALU.logical_shift_right),
                    deps=[d1])
                uops = []
                for i, plane in enumerate((t_lo, t_hi)):
                    pdep = o1 if i == 0 else o2
                    q1 = p.add("vector", lambda v, plane=plane: v.tensor_single_scalar(
                        out=t_u[:, 0:W], in_=plane[:, 0:W], scalar=7, op=ALU.bitwise_and),
                        deps=[pdep])
                    q2 = p.add("vector", lambda v, plane=plane: v.tensor_single_scalar(
                        out=t_sg[:, 0:W], in_=plane[:, 0:W], scalar=8, op=ALU.bitwise_and),
                        deps=[pdep])
                    q3 = p.add("vector", lambda v: v.tensor_scalar(
                        out=t_a[:, 0:W], in0=t_u[:, 0:W], scalar1=64, scalar2=0x3F00,
                        op0=ALU.mult, op1=ALU.add), deps=[q1])
                    q4 = p.add("vector", lambda v: v.tensor_single_scalar(
                        out=t_b[:, 0:W], in_=t_u[:, 0:W], scalar=0x3F00, op=ALU.mult),
                        deps=[q1])
                    q5 = p.add("vector", lambda v: v.tensor_tensor(
                        out=t_a[:, 0:W], in0=t_a[:, 0:W], in1=t_b[:, 0:W], op=ALU.min),
                        deps=[q3, q4])
                    q6 = p.add("vector", lambda v: v.scalar_tensor_tensor(
                        out=t_a[:, 0:W], in0=t_sg[:, 0:W], scalar=4096, in1=t_a[:, 0:W],
                        op0=ALU.mult, op1=ALU.add), deps=[q2, q5], se=True)
                    q7 = p.add("vector", lambda v, i=i: v.tensor_copy(
                        out=u16t[:, 0:2 * W].rearrange("p (w two) -> p w two", two=2)[:, :, i],
                        in_=t_a[:, 0:W]), deps=[q6], se=True)
                    uops.append(q7)
                o_sc = p.add("vector", lambda v: v.tensor_tensor(
                    out=dqb[:, 0:8, :].rearrange("p a (s e) -> p (a s) e", e=32),
                    in0=u16t[:, 0:2048].bitcast(BF16).rearrange("p (b e) -> p b e", e=32),
                    in1=sct[:, 0:8, :].rearrange("p a s -> p (a s)").unsqueeze(2)
                        .broadcast_to((128, 64, 32)),
                    op=ALU.mult), deps=uops + [o_half], se=True)
                dma_dep = [o_sc]
                evs = []
                dqf = dqb[:, 0:8, :].rearrange("p a w -> p (a w)")
                for a in range(16):
                    tr1 = p.add("tensor", lambda t, a=a: t.transpose(
                        out=ps_t[:, 0:128], in_=dqf[:, a * 128:(a + 1) * 128],
                        identity=identt),
                        deps=[o_sc, d_id] + ([last_ev] if last_ev is not None else []))
                    ev = p.add("vector", lambda v, a=a, sel=sel: v.tensor_copy(
                        out=stg_dn[:, sel, a, :], in_=ps_t[:, 0:128]),
                        deps=[tr1, stg_war[sel]])
                    last_ev = ev
                    evs.append(ev)
                st = p.add(STAGE_DMA_Q, lambda e, g=g, sel=sel: e.dma_start(
                    out=out_v[:, :, g * 128:(g + 1) * 128],
                    in_=stg_dn[:, sel, :, :]), deps=evs, dma=True)
                stg_war[sel] = st
                st_ops.append(st)
            return st_ops

        def transpose_cast(w_dram, O, ncols, out_dram, scale=None, after=(),
                           bufs=None):
            wldb, wcbb, stgb = bufs
            otile = O // 128
            in_v = w_dram.rearrange("(o q) c -> q o c", q=128)
            out_v = out_dram.rearrange("(a q) n -> q a n", q=128)
            d1 = p.add("sync", lambda e: e.dma_start(
                out=wldb[:, 0:otile, 0:ncols], in_=in_v), dma=True, deps=list(after))
            if scale is None:
                oc = p.add("vector", lambda v: v.tensor_copy(
                    out=wcbb[:, 0:otile, 0:ncols], in_=wldb[:, 0:otile, 0:ncols]),
                    deps=[d1])
            else:
                oc = p.add("vector", lambda v: v.tensor_single_scalar(
                    out=wcbb[:, 0:otile, 0:ncols], in_=wldb[:, 0:otile, 0:ncols],
                    scalar=scale, op=ALU.mult), deps=[d1])
            sts = []
            last_ev = None
            stg_war = [None, None]
            for ot in range(otile):
                sel = ot % 2
                evs = []
                for cb in range(ncols // 128):
                    tr = p.add("tensor", lambda t, ot=ot, cb=cb: t.transpose(
                        out=ps_t[:, 0:128], in_=wcbb[:, ot, cb * 128:(cb + 1) * 128],
                        identity=identt),
                        deps=[oc, d_id] + ([last_ev] if last_ev is not None else []))
                    ev = p.add("vector", lambda v, cb=cb, sel=sel: v.tensor_copy(
                        out=stgb[:, sel, cb, 0:128],
                        in_=ps_t[:, 0:128]), deps=[tr, stg_war[sel]])
                    last_ev = ev
                    evs.append(ev)
                st = p.add(STAGE_DMA_Q, lambda e, ot=ot, sel=sel: e.dma_start(
                    out=out_v[:, 0:ncols // 128, ot * 128:(ot + 1) * 128],
                    in_=stgb[:, sel, 0:ncols // 128, 0:128]), deps=evs, dma=True)
                stg_war[sel] = st
                sts.append(st)
            return sts

        g_sts = dequant_gu(g_pk, g_sc, gT_s)
        if UPTO == "A2":
            o1 = p.add("gpsimd", lambda g: g.collective_compute(
                "AllGather", ALU.bypass, replica_groups=RG_ALL,
                ins=[gT_s.opt()], outs=[gT.opt()]), deps=g_sts, cc=True)
            o2 = p.add("gpsimd", lambda g: g.collective_compute(
                "AllGather", ALU.bypass, replica_groups=RG_ALL,
                ins=[gT_s.opt()], outs=[uT.opt()]), deps=g_sts, cc=True)
            st = p.add("gpsimd", lambda e: e.dma_start(
                out=x_out.rearrange("(ft p) t -> p ft t", p=128), in_=B["xt"]),
                deps=[x_ld, o1, o2], dma=True)
            p.build(stack)
            return nc
        u_sts = dequant_gu(u_pk, u_sc, uT_s, after=g_sts[-1:])
        d_sts = dequant_down(after=u_sts[-1:])

        cc_count = [0]

        def ag(ins_ap, outs_ap, rg, deps):
            cc_count[0] += 1
            if NO_CC or cc_count[0] > CC_LIMIT:
                return None
            return p.add("gpsimd", lambda g: g.collective_compute(
                "AllGather", ALU.bypass, replica_groups=rg,
                ins=[ins_ap.opt()], outs=[outs_ap.opt()]), deps=deps, cc=True)

        o_ag_g = ag(gT_s, gT, RG_ALL, g_sts)
        o_ag_u = ag(uT_s, uT, RG_ALL, u_sts)
        o_ag_d = ag(dT_s, dT, RG_ALL, d_sts)
        if DEBUG:
            p.add("scalar", lambda e: e.dma_start(
                out=DBG["dbg_dts"], in_=dT_s), deps=d_sts, dma=True)
            p.add("scalar", lambda e: e.dma_start(
                out=DBG["dbg_dtg"], in_=dT[0:2048, :]), deps=[o_ag_d], dma=True)

        def emit_ffn0(x_dep, prior_psum_readers):
            SC.reset()
            gbuf = SC.tile("gbuf", [128, NFT, 512], BF16)
            ubuf = SC.tile("ubuf", [128, NFT, 512], BF16)
            dbuf = SC.tile("dbuf", [128, 4, 2048], BF16)
            ibuf = SC.tile("ibuf", [128, 4, TC], BF16)
            t_s2 = SC.tile("t_s2", [128, TC], F32)
            t_a = SC.tile("t_a2", [128, TC], F32)
            t_sgl = SC.tile("t_sgl", [128, TC], F32)
            t_p = SC.tile("t_p", [128, TC], F32)
            lnx = B["lnx"]
            xt = B["xt"]

            ln_ops, psa_readers = emit_ln(p, B, x_dep, extra_rdeps=consts,
                                          psum_war=prior_psum_readers)
            gT_v = gT.rearrange("(ft q) n -> q ft n", q=128)
            uT_v = uT.rearrange("(ft q) n -> q ft n", q=128)
            dT_v = dT.rearrange("(nt q) h -> q nt h", q=128)
            gate_ps = ps_a[:, 0:TC]
            up_ps = ps_a[:, TC:2 * TC]
            dslots = [ps_b[:, 0:512], ps_b[:, 512:1024]]
            dslot_war = [psa_readers, psa_readers]

            NCH = 32
            CH = 512
            NSUB = 4
            prev_panel_readers = [o_ag_d]
            last_gate_user = {0: psa_readers, 1: psa_readers}
            x_accum = dict(x_dep)
            ln_dep_all = list(ln_ops)
            dsl = 0
            for c in range(NCH):
                dg = p.add("sync", lambda e, c=c: e.dma_start(
                    out=gbuf, in_=gT_v[:, :, c * CH:(c + 1) * CH]),
                    deps=prev_panel_readers + [o_ag_g], dma=True)
                du = p.add("sync", lambda e, c=c: e.dma_start(
                    out=ubuf, in_=uT_v[:, :, c * CH:(c + 1) * CH]),
                    deps=prev_panel_readers + [o_ag_u], dma=True)
                dd = p.add("sync", lambda e, c=c: e.dma_start(
                    out=dbuf, in_=dT_v[:, c * NSUB:(c + 1) * NSUB, :]),
                    deps=prev_panel_readers + [o_ag_d], dma=True)
                panel_readers = []
                inter_ready = []
                for s in range(NSUB):
                    def mm_gate(t, s=s):
                        for h in (0, 1):
                            for ft in range(NFT):
                                inst = t.matmul(out=gate_ps[:, h * 512:(h + 1) * 512],
                                                lhsT=gbuf[:, ft, s * 128:(s + 1) * 128],
                                                rhs=lnx[:, ft, h * 512:(h + 1) * 512],
                                                start=(ft == 0), stop=(ft == NFT - 1))
                        return inst

                    def mm_up(t, s=s):
                        for h in (0, 1):
                            for ft in range(NFT):
                                inst = t.matmul(out=up_ps[:, h * 512:(h + 1) * 512],
                                                lhsT=ubuf[:, ft, s * 128:(s + 1) * 128],
                                                rhs=lnx[:, ft, h * 512:(h + 1) * 512],
                                                start=(ft == 0), stop=(ft == NFT - 1))
                        return inst
                    og = p.add("tensor", mm_gate, deps=[dg] + ln_dep_all + last_gate_user[0])
                    ou = p.add("tensor", mm_up, deps=[du] + ln_dep_all + last_gate_user[1])
                    panel_readers += [og, ou]
                    o1 = p.add("scalar", lambda s_, og=og: s_.activation(
                        out=t_s2, in_=gate_ps, func=AF.Square), deps=[og])
                    o2 = p.add("vector", lambda v, og=og: v.scalar_tensor_tensor(
                        out=t_a, in0=t_s2, scalar=GELU_ADD, in1=gate_ps,
                        op0=ALU.add, op1=ALU.mult), deps=[o1, og])
                    o3 = p.add("scalar", lambda s_: s_.activation(
                        out=t_sgl, in_=t_a, func=AF.Sigmoid, scale=GELU_SCALE), deps=[o2])
                    o4 = p.add("vector", lambda v, og=og: v.tensor_tensor(
                        out=t_p, in0=t_sgl, in1=gate_ps, op=ALU.mult), deps=[o3, og])
                    o5 = p.add("vector", lambda v, s=s, ou=ou: v.tensor_tensor(
                        out=ibuf[:, s], in0=t_p, in1=up_ps, op=ALU.mult), deps=[o4, ou])
                    inter_ready.append(o5)
                    last_gate_user[0] = [o4]
                    last_gate_user[1] = [o5]
                for ht in range(NFT):
                    for h in (0, 1):
                        slot = dslots[dsl % 2]
                        mywar = dslot_war[dsl % 2]

                        def mm_down(t, ht=ht, h=h, slot=slot):
                            for s in range(NSUB):
                                inst = t.matmul(out=slot,
                                                lhsT=dbuf[:, s, ht * 128:(ht + 1) * 128],
                                                rhs=ibuf[:, s, h * 512:(h + 1) * 512],
                                                start=(s == 0), stop=(s == NSUB - 1))
                            return inst
                        om = p.add("tensor", mm_down, deps=[dd] + inter_ready + mywar)
                        panel_readers.append(om)
                        oe = p.add("vector", lambda v, ht=ht, h=h, slot=slot: v.tensor_tensor(
                            out=xt[:, ht, h * 512:(h + 1) * 512],
                            in0=xt[:, ht, h * 512:(h + 1) * 512],
                            in1=slot, op=ALU.add),
                            deps=[om] + x_accum[ht])
                        x_accum[ht] = [oe] if h == 1 else x_accum[ht] + [oe]
                        dslot_war[dsl % 2] = [oe]
                        dsl += 1
                prev_panel_readers = panel_readers
            xdep = {ft: x_accum[ft] for ft in range(NFT)}
            psum_readers = dslot_war[0] + dslot_war[1] + \
                last_gate_user[0] + last_gate_user[1]
            return xdep, psum_readers

        def dump_x(tag, xd):
            if not DEBUG:
                return
            t = DBG[tag]
            p.add(STAGE_DMA_Q, lambda e, t=t: e.dma_start(
                out=t.rearrange("(ft p) t -> p ft t", p=128), in_=B["xt"]),
                deps=sum((list(v) for v in xd.values()), []), dma=True)

        def finish(extra_deps, xd):
            p.add("gpsimd", lambda e: e.dma_start(
                out=x_out.rearrange("(ft p) t -> p ft t", p=128), in_=B["xt"]),
                deps=sum((list(v) for v in xd.values()), []) + extra_deps, dma=True)
            p.build(stack)

        if UPTO == "A":
            finish([o_ag_g, o_ag_u, o_ag_d], x_dep)
            return nc

        xdep, psr = emit_ffn0(x_dep, [])
        dump_x("dbg_f0a", xdep)
        if UPTO == "B":
            finish([o_ag_g, o_ag_u, o_ag_d], xdep)
            return nc

        p.barrier()
        SC.reset()
        wld2 = SC.tile("wld2", [128, 16, 256], F32)
        wcb2 = SC.tile("wcb2", [128, 16, 256], BF16)
        stg2t = SC.tile("stg2t", [128, 2, 2, 1024], BF16)
        TB = (wld2, wcb2, stg2t)
        prev = []
        w1_sts = []
        for rc in range(2):
            w1_sts += transpose_cast(w1_s[rc * 2048:(rc + 1) * 2048, :], 2048, 256,
                                     w1T_s[:, rc * 2048:(rc + 1) * 2048], after=prev,
                                     bufs=TB)
            prev = w1_sts[-1:]
        w2_sts = []
        for cc in range(2):
            for rc in range(2):
                w2_sts += transpose_cast(
                    w2_s[rc * 2048:(rc + 1) * 2048, cc * 256:(cc + 1) * 256], 2048, 256,
                    w2T_s[cc * 256:(cc + 1) * 256, rc * 2048:(rc + 1) * 2048],
                    after=prev, bufs=TB)
                prev = w2_sts[-1:]
        w3_sts = []
        for cc in range(2):
            w3_sts += transpose_cast(w3_s[:, cc * 256:(cc + 1) * 256], 2048, 256,
                                     w3T_s[cc * 256:(cc + 1) * 256, :], scale=0.5,
                                     after=prev, bufs=TB)
            prev = w3_sts[-1:]
        qkv_sts = []
        for rc in range(3):
            qkv_sts += transpose_cast(qkv_s[rc * 2048:(rc + 1) * 2048, :], 2048, 256,
                                      qkvT_s[:, rc * 2048:(rc + 1) * 2048], after=prev,
                                      bufs=TB)
            prev = qkv_sts[-1:]
        proj_sts = transpose_cast(proj_s, 2048, 256, projT_s, after=prev, bufs=TB)

        SC.reset(32 * 1024)
        wpan = SC.tile("wpan", [128, NFT, 256], F32)
        wpanb = SC.tile("wpanb", [128, NFT, 256], BF16)
        brow_f = SC.tile("brow_f", [1, D], F32)
        brow_b = SC.tile("brow_b", [1, D], BF16)
        stg = SC.tile("stg", [128, 2, TC], BF16)
        onesrow_b = B["onesrow_b"]
        lnx = B["lnx"]

        ln_ops, psa_readers = emit_ln(p, B, xdep, extra_rdeps=consts, psum_war=psr)
        psum_war = {0: psa_readers, 1: psa_readers}
        wpan_war = []
        out_dma = []
        kv_done = {"k0d": [], "v0d": []}
        q_done = []
        for wi, (w, bsrc, out_dram, key) in enumerate(
                ((wq, bq, q0d, "q"), (wk, bk, k0d, "k0d"), (wv, bv, v0d, "v0d"))):
            d_b = p.add("sync", lambda e, bsrc=bsrc: e.dma_start(out=brow_f, in_=bsrc),
                        dma=True, deps=[wpan_war[0]] if wpan_war else [])
            o_bc = p.add("vector", lambda v: v.tensor_copy(out=brow_b, in_=brow_f),
                         deps=[d_b], se=True)
            for pan in range(8):
                dw = p.add("sync", lambda e, w=w, pan=pan: e.dma_start(
                    out=wpan, in_=w.rearrange("(ft q) m -> q ft m", q=128)
                    [:, :, pan * 256:(pan + 1) * 256]), dma=True, deps=wpan_war)
                cw = p.add("vector", lambda v: v.tensor_copy(out=wpanb, in_=wpan), deps=[dw])
                for mt in range(2):
                    m = pan * 2 + mt
                    sel = m % 2
                    mm_ps = ps_a[:, sel * TC:(sel + 1) * TC]

                    def mm(t, mt=mt, m=m, mm_ps=mm_ps):
                        for h in (0, 1):
                            for ft in range(NFT):
                                t.matmul(out=mm_ps[:, h * 512:(h + 1) * 512],
                                         lhsT=wpanb[:, ft, mt * 128:(mt + 1) * 128],
                                         rhs=lnx[:, ft, h * 512:(h + 1) * 512],
                                         start=(ft == 0), stop=False)
                            inst = t.matmul(out=mm_ps[:, h * 512:(h + 1) * 512],
                                            lhsT=brow_b[:, m * 128:(m + 1) * 128],
                                            rhs=onesrow_b[:, h * 512:(h + 1) * 512],
                                            start=False, stop=True)
                        return inst
                    om = p.add("tensor", mm, deps=[cw, o_bc] + ln_ops + psum_war[sel])
                    oe = p.add("vector", lambda v, sel=sel, mm_ps=mm_ps: v.tensor_copy(
                        out=stg[:, sel], in_=mm_ps), deps=[om] + out_dma[-2:])
                    od = p.add("gpsimd", lambda e, m=m, sel=sel, out_dram=out_dram:
                               e.dma_start(out=out_dram[m * 128:(m + 1) * 128, :],
                                           in_=stg[:, sel]), deps=[oe], dma=True)
                    psum_war[sel] = [oe]
                    out_dma.append(od)
                    if key == "q":
                        q_done.append(od)
                    else:
                        kv_done[key].append(od)
                wpan_war = [om]
        qkv0_ps_readers = psum_war[0] + psum_war[1]

        o_ag_k0 = ag(k0d, k0g, RG_PAIR, kv_done["k0d"])
        o_ag_v0 = ag(v0d, v0g, RG_PAIR, kv_done["v0d"])
        if UPTO == "C":
            finish([o_ag_k0, o_ag_v0], xdep)
            return nc

        p.barrier()
        SC.reset(32 * 1024)
        q_sb = SC.tile("q_sb", [128, TC], BF16)
        k_sb = SC.tile("k_sb", [128, 2048], BF16)
        vf_sb = SC.tile("vf_sb", [128, 2048], BF16)
        v_sb = SC.tile("v_sb", [128, 16, 128], BF16)
        p_sb = SC.tile("p_sb", [128, 2, 2048], BF16)
        pt_sb = SC.tile("pt_sb", [128, 2, 128], BF16)
        rmax = SC.tile("rmax", [128, 2], F32)
        negm = SC.tile("negm", [128, 2], F32)
        rsum = SC.tile("rsum", [128, 2], F32)
        rcp = SC.tile("rcp", [128, 2], F32)
        ctx_sb = B["lnx"]
        ps_sc = ps_a
        ps_ctx = ps_b[:, 0:128]

        sc_war = {0: qkv0_ps_readers, 1: qkv0_ps_readers}
        tr_war = [None, None]
        ppwar = [None, None]
        ctx_war = list(ln_ops)
        kv_war = []
        ctx_ops_attn = []
        for hd in range(ATTN_NH):
            d_q = p.add("sync", lambda e, hd=hd: e.dma_start(
                out=q_sb, in_=q0d[hd * 128:(hd + 1) * 128, :]), dma=True,
                deps=q_done[-1:] + kv_war)
            d_k0 = p.add("sync", lambda e, hd=hd: e.dma_start(
                out=k_sb[:, 0:1024], in_=k0g[hd * 128:(hd + 1) * 128, :]), dma=True,
                deps=[o_ag_k0] + kv_war)
            d_k1 = p.add("sync", lambda e, hd=hd: e.dma_start(
                out=k_sb[:, 1024:2048], in_=k0g[2048 + hd * 128:2048 + (hd + 1) * 128, :]),
                dma=True, deps=[o_ag_k0] + kv_war)
            d_v0 = p.add("sync", lambda e, hd=hd: e.dma_start(
                out=vf_sb[:, 0:1024], in_=v0g[hd * 128:(hd + 1) * 128, :]), dma=True,
                deps=[o_ag_v0] + kv_war)
            d_v1 = p.add("sync", lambda e, hd=hd: e.dma_start(
                out=vf_sb[:, 1024:2048], in_=v0g[2048 + hd * 128:2048 + (hd + 1) * 128, :]),
                dma=True, deps=[o_ag_v0] + kv_war)
            v_evs = []
            if ATTN_PARTS == "dma":
                kv_war = [d_q, d_k0, d_k1, d_v0, d_v1]
                head_ce = [d_v1]
                ctx_ops_attn.append(d_v1)
                continue
            for kt in range(16):
                sel = kt % 2
                o_tr = p.add("tensor", lambda t, kt=kt, sel=sel: t.transpose(
                    out=ps_t[:, sel * 1024:sel * 1024 + 128],
                    in_=vf_sb[:, kt * 128:(kt + 1) * 128], identity=identt),
                    deps=[d_v0, d_v1, d_id, tr_war[sel]])
                o_ve = p.add("vector", lambda v, kt=kt, sel=sel: v.tensor_copy(
                    out=v_sb[:, kt, :], in_=ps_t[:, sel * 1024:sel * 1024 + 128]),
                    deps=[o_tr] + v_evs[-2:-1])
                tr_war[sel] = o_ve
                v_evs.append(o_ve)
            kv_war = []
            head_ce = []
            if ATTN_PARTS == "vtr":
                kv_war = v_evs[-1:]
                ctx_ops_attn.append(v_evs[-1])
                continue
            for qt in range(8):
                r = qt % 2
                nk = 1024 + (qt + 1) * 128
                pp = p_sb[:, r]

                def mm_sc(t, qt=qt, nk=nk):
                    for kc in range(0, nk, 512):
                        w = min(512, nk - kc)
                        inst = t.matmul(out=ps_sc[:, kc:kc + w],
                                        lhsT=q_sb[:, qt * 128:(qt + 1) * 128],
                                        rhs=k_sb[:, kc:kc + w], start=True, stop=True)
                    return inst
                o_sc = p.add("tensor", mm_sc, deps=[d_q, d_k0, d_k1] + sc_war[0] + sc_war[1])
                if ATTN_PARTS == "score":
                    sc_war[0] = [o_sc]
                    sc_war[1] = [o_sc]
                    head_ce.append(o_sc)
                    ctx_ops_attn.append(o_sc)
                    continue
                o_max = p.add("vector", lambda v, nk=nk, r=r: v.reduce_max(
                    out=rmax[:, r:r + 1], in_=ps_sc[:, 0:nk], axis=AX.X), deps=[o_sc],
                    se=True)
                o_ng = p.add("vector", lambda v, r=r: v.tensor_single_scalar(
                    out=negm[:, r:r + 1], in_=rmax[:, r:r + 1], scalar=-ATT_SC,
                    op=ALU.mult), deps=[o_max], se=True)
                o_exp = p.add("scalar", lambda s, nk=nk, r=r, pp=pp: s.activation(
                    out=pp[0:128, 0:nk], in_=ps_sc[:, 0:nk], func=AF.Exp,
                    scale=ATT_SC, bias=negm[:, r:r + 1]),
                    deps=[o_ng, o_sc, ppwar[r]], se=True)
                sc_war[0] = [o_exp]
                sc_war[1] = [o_exp]
                o_mA = p.add("vector", lambda v, qt=qt, pp=pp: v.tensor_tensor(
                    out=pp[0:128, qt * 128:(qt + 1) * 128],
                    in0=pp[0:128, qt * 128:(qt + 1) * 128], in1=maskA, op=ALU.mult),
                    deps=[o_exp, d_mA], se=True)
                o_mC = p.add("vector", lambda v, qt=qt, pp=pp: v.tensor_tensor(
                    out=pp[0:128, (qt + 1) * 128:(qt + 1) * 128 + 1024],
                    in0=pp[0:128, (qt + 1) * 128:(qt + 1) * 128 + 1024], in1=maskC,
                    op=ALU.mult), deps=[o_exp, d_mC], se=True)
                o_mB = p.add("vector", lambda v, qt=qt, pp=pp: v.tensor_tensor(
                    out=pp[0:128, 1024 + qt * 128:1024 + (qt + 1) * 128],
                    in0=pp[0:128, 1024 + qt * 128:1024 + (qt + 1) * 128], in1=maskB,
                    op=ALU.mult), deps=[o_mC, d_mB], se=True)
                o_sum = p.add("vector", lambda v, nk=nk, r=r, pp=pp: v.reduce_sum(
                    out=rsum[:, r:r + 1], in_=pp[0:128, 0:nk], axis=AX.X),
                    deps=[o_mA, o_mB], se=True)
                o_rcp = p.add("vector", lambda v, r=r: v.reciprocal(
                    out=rcp[:, r:r + 1], in_=rsum[:, r:r + 1]), deps=[o_sum], se=True)
                o_nrm = p.add("vector", lambda v, nk=nk, r=r, pp=pp: v.tensor_single_scalar(
                    out=pp[0:128, 0:nk], in_=pp[0:128, 0:nk], scalar=rcp[:, r:r + 1],
                    op=ALU.mult), deps=[o_rcp], se=True)
                if ATTN_PARTS == "soft":
                    ppwar[r] = o_nrm
                    sc_war[0] = [o_nrm]
                    sc_war[1] = [o_nrm]
                    head_ce.append(o_nrm)
                    ctx_ops_attn.append(o_nrm)
                    continue
                o_cm = None
                nkt = 8 + qt + 1
                for kt in range(nkt):
                    sel = kt % 2
                    o_tr = p.add("tensor", lambda t, kt=kt, sel=sel, pp=pp: t.transpose(
                        out=ps_t[:, sel * 1024:sel * 1024 + 128],
                        in_=pp[0:128, kt * 128:(kt + 1) * 128], identity=identt),
                        deps=[o_nrm, d_id, tr_war[sel]])
                    o_pe = p.add("vector", lambda v, sel=sel: v.tensor_copy(
                        out=pt_sb[:, sel], in_=ps_t[:, sel * 1024:sel * 1024 + 128]),
                        deps=[o_tr])
                    cdeps = [o_pe] + v_evs
                    if kt == 0:
                        cdeps += ctx_war
                    o_cm = p.add("tensor", lambda t, kt=kt, sel=sel, nkt=nkt: t.matmul(
                        out=ps_ctx, lhsT=v_sb[:, kt, :], rhs=pt_sb[:, sel],
                        start=(kt == 0), stop=(kt == nkt - 1)), deps=cdeps)
                    tr_war[sel] = o_cm
                    if kt == nkt - 1:
                        ppwar[r] = o_tr
                o_ce = p.add("vector", lambda v, hd=hd, qt=qt: v.tensor_copy(
                    out=ctx_sb[:, hd, qt * 128:(qt + 1) * 128], in_=ps_ctx), deps=[o_cm])
                ctx_war = [o_ce]
                head_ce.append(o_ce)
                ctx_ops_attn.append(o_ce)
            kv_war = head_ce[-1:]

        if DEBUG:
            p.add(STAGE_DMA_Q, lambda e: e.dma_start(
                out=DBG["dbg_ctx"].rearrange("(ft p) t -> p ft t", p=128),
                in_=B["lnx"]), deps=list(ctx_ops_attn), dma=True)
        o_ag_w1 = ag(w1T_s, w1T, RG_ALL, w1_sts)
        o_ag_w2 = ag(w2T_s, w2T, RG_ALL, w2_sts)
        o_ag_w3 = ag(w3T_s, w3T, RG_ALL, w3_sts)
        o_ag_qkv = ag(qkvT_s, qkvT, RG_ALL, qkv_sts)
        o_ag_proj = ag(projT_s, projT, RG_ALL, proj_sts)
        if UPTO == "D":
            finish(ctx_ops_attn[-1:], xdep)
            return nc

        p.barrier()
        SC.reset(32 * 1024)
        wpanP = SC.tile("wpan0", [128, NFT, 256], F32)
        wpanbP = SC.tile("wpanb0", [128, NFT, 256], BF16)
        brow_fP = SC.tile("brow_f0", [1, D], F32)
        brow_bP = SC.tile("brow_b0", [1, D], BF16)
        xt = B["xt"]
        d_bo = p.add("sync", lambda e: e.dma_start(out=brow_fP, in_=bo), dma=True)
        o_bc = p.add("vector", lambda v: v.tensor_copy(out=brow_bP, in_=brow_fP),
                     deps=[d_bo], se=True)
        psum_war = {0: sc_war[0], 1: sc_war[1]}
        wpan_war = []
        x_done = {}
        for pan in range(8):
            dw = p.add("sync", lambda e, pan=pan: e.dma_start(
                out=wpanP, in_=wo.rearrange("(ft q) m -> q ft m", q=128)
                [:, :, pan * 256:(pan + 1) * 256]), dma=True, deps=wpan_war)
            cw = p.add("vector", lambda v: v.tensor_copy(out=wpanbP, in_=wpanP), deps=[dw])
            for mt in range(2):
                m = pan * 2 + mt
                sel = m % 2
                mm_ps = ps_a[:, sel * TC:(sel + 1) * TC]

                def mm(t, mt=mt, m=m, mm_ps=mm_ps):
                    for h in (0, 1):
                        for ft in range(NFT):
                            t.matmul(out=mm_ps[:, h * 512:(h + 1) * 512],
                                     lhsT=wpanbP[:, ft, mt * 128:(mt + 1) * 128],
                                     rhs=ctx_sb[:, ft, h * 512:(h + 1) * 512],
                                     start=(ft == 0), stop=False)
                        inst = t.matmul(out=mm_ps[:, h * 512:(h + 1) * 512],
                                        lhsT=brow_bP[:, m * 128:(m + 1) * 128],
                                        rhs=onesrow_b[:, h * 512:(h + 1) * 512],
                                        start=False, stop=True)
                    return inst
                om = p.add("tensor", mm,
                           deps=[cw, o_bc] + ctx_ops_attn[-1:] + consts + psum_war[sel])
                oa = p.add("vector", lambda v, m=m, mm_ps=mm_ps: v.tensor_tensor(
                    out=xt[:, m], in0=xt[:, m], in1=mm_ps, op=ALU.add),
                    deps=[om] + xdep[m])
                psum_war[sel] = [oa]
                x_done[m] = [oa]
            wpan_war = [om]

        dump_x("dbg_p0", x_done)

        p.barrier()
        xdep, psr = emit_ffn0(x_done, psum_war[0] + psum_war[1])
        dump_x("dbg_f0b", xdep)
        if UPTO == "E":
            finish([], xdep)
            return nc

        def emit_ffn1(x_dep, prior_psum_readers, store_out=False):
            SC.reset()
            h1 = SC.tile("h1", [128, 32, 512], BF16)
            wpan = SC.tile("wpanf1", [128, 32, 2, 128], BF16)
            btmp_f = SC.tile("btmp_f", [1, 512], F32)
            b12_b = SC.tile("b12_b", [1, 2, 4096], BF16)
            b3row_b = SC.tile("b3row_b", [1, D], BF16)
            lnx = B["lnx"]
            xt = B["xt"]
            h2h = lnx.rearrange("p a t -> p (a t)").rearrange("p (a t) -> p a t", a=32)
            onesrow_b = B["onesrow_b"]

            w1T_v = w1T.rearrange("(ft q) m -> q ft m", q=128)
            w2T_v = w2T.rearrange("(kt q) m -> q kt m", q=128)
            w3T_v = w3T.rearrange("(kt q) m -> q kt m", q=128)
            wpan_war = [None, None]
            x_done = {ft: list(x_dep[ft]) for ft in range(NFT)}
            out_sts = []
            psr = list(prior_psum_readers)
            o_b1 = o_b2 = o_b3 = None
            for hh in (0, 1):
                p.barrier()
                ln_ops, psa_readers = emit_ln(p, B, x_done, extra_rdeps=consts,
                                              psum_war=psr)
                if hh == 0:
                    o_b1 = emit_bias_cast(p, btmp_f, b12_b[:, 0], b1, 4096)
                    o_b2 = emit_bias_cast(p, btmp_f, b12_b[:, 1], b2, 4096, after=[o_b1])
                    o_b3 = emit_bias_cast(p, btmp_f, b3row_b, b3, D, scale=0.5,
                                          after=[o_b2])
                psum_war = {i: psa_readers for i in range(4)}
                h1_ready = []
                for m in range(32):
                    sel = m % 2
                    dw = p.add("sync", lambda e, m=m, sel=sel: e.dma_start(
                        out=wpan[:, 0:NFT, sel, :], in_=w1T_v[:, :, m * 128:(m + 1) * 128]),
                        dma=True, deps=[wpan_war[sel], o_ag_w1])
                    mm_ps = ps_a[:, sel * 512:(sel + 1) * 512]

                    def mm(t, m=m, sel=sel, hh=hh, mm_ps=mm_ps):
                        for ft in range(NFT):
                            t.matmul(out=mm_ps, lhsT=wpan[:, ft, sel, :],
                                     rhs=lnx[:, ft, hh * 512:(hh + 1) * 512],
                                     start=(ft == 0), stop=False)
                        return t.matmul(out=mm_ps, lhsT=b12_b[:, 0, m * 128:(m + 1) * 128],
                                        rhs=onesrow_b[:, 0:512], start=False, stop=True)
                    om = p.add("tensor", mm, deps=[dw, o_b1] + ln_ops + psum_war[sel])
                    oe = p.add("scalar", lambda s, m=m, mm_ps=mm_ps: s.activation(
                        out=h1[:, m], in_=mm_ps, func=AF.Relu), deps=[om])
                    psum_war[sel] = [oe]
                    wpan_war[sel] = om
                    h1_ready.append(oe)
                h2_ready = []
                h2_war = list(ln_ops)
                for m in range(32):
                    sel = 2 + m % 2
                    dw = p.add("sync", lambda e, m=m, sel=sel: e.dma_start(
                        out=wpan[:, :, sel - 2, :], in_=w2T_v[:, :, m * 128:(m + 1) * 128]),
                        dma=True, deps=[wpan_war[sel - 2], o_ag_w2])
                    mm_ps = ps_a[:, sel * 512:(sel + 1) * 512]

                    def mm(t, m=m, sel=sel, mm_ps=mm_ps):
                        for kt in range(32):
                            t.matmul(out=mm_ps, lhsT=wpan[:, kt, sel - 2, :],
                                     rhs=h1[:, kt, :], start=(kt == 0), stop=False)
                        return t.matmul(out=mm_ps, lhsT=b12_b[:, 1, m * 128:(m + 1) * 128],
                                        rhs=onesrow_b[:, 0:512], start=False, stop=True)
                    om = p.add("tensor", mm, deps=[dw, o_b2] + h1_ready + psum_war[sel])
                    oe = p.add("scalar", lambda s, m=m, mm_ps=mm_ps: s.activation(
                        out=h2h[:, m, 0:512], in_=mm_ps, func=AF.Relu),
                        deps=[om] + h2_war)
                    psum_war[sel] = [oe]
                    wpan_war[sel - 2] = om
                    h2_ready.append(oe)
                w3_psum_war = [None, None]
                for fo in range(NFT):
                    sel = fo % 2
                    dw = p.add("sync", lambda e, fo=fo, sel=sel: e.dma_start(
                        out=wpan[:, :, sel, :], in_=w3T_v[:, :, fo * 128:(fo + 1) * 128]),
                        dma=True, deps=[wpan_war[sel], o_ag_w3])
                    mm_ps = ps_b[:, sel * 512:(sel + 1) * 512]

                    def mm(t, fo=fo, sel=sel, mm_ps=mm_ps):
                        for kt in range(32):
                            t.matmul(out=mm_ps, lhsT=wpan[:, kt, sel, :],
                                     rhs=h2h[:, kt, 0:512], start=(kt == 0), stop=False)
                        return t.matmul(out=mm_ps,
                                        lhsT=b3row_b[:, fo * 128:(fo + 1) * 128],
                                        rhs=onesrow_b[:, 0:512], start=False, stop=True)
                    om = p.add("tensor", mm,
                               deps=[dw, o_b3, w3_psum_war[sel]] + h2_ready)
                    oa = p.add("vector", lambda v, fo=fo, hh=hh, sel=sel, mm_ps=mm_ps:
                               v.tensor_tensor(
                        out=xt[:, fo, hh * 512:(hh + 1) * 512],
                        in0=xt[:, fo, hh * 512:(hh + 1) * 512],
                        in1=mm_ps, op=ALU.add), deps=[om] + x_done[fo])
                    x_done[fo] = [oa]
                    w3_psum_war[sel] = oa
                    wpan_war[sel] = om
                    if store_out:
                        st = p.add("gpsimd", lambda e, fo=fo, hh=hh: e.dma_start(
                            out=x_out.rearrange("(ft p) t -> p ft t", p=128)
                            [:, fo, hh * 512:(hh + 1) * 512],
                            in_=xt[:, fo, hh * 512:(hh + 1) * 512]), deps=[oa], dma=True)
                        out_sts.append(st)
                psr = sum((psum_war[i] for i in range(4)), []) + \
                    [w3_psum_war[0], w3_psum_war[1]]
            return x_done, psr, out_sts

        p.barrier()
        xdep, psr, _ = emit_ffn1(xdep, psr)
        dump_x("dbg_f1a", xdep)
        if UPTO == "F":
            finish([], xdep)
            return nc

        p.barrier()
        SC.reset()
        wpanb1 = SC.tile("wpanb1", [128, NFT, 512], BF16)
        stg1 = SC.tile("stg1", [128, 2, TC], BF16)
        lnx = B["lnx"]
        ln_ops, psa_readers = emit_ln(p, B, xdep, extra_rdeps=consts, psum_war=psr)
        qkvT_v = qkvT.rearrange("(ft q) m -> q ft m", q=128)
        psum_war = {0: psa_readers, 1: psa_readers}
        wpan_war = []
        out_dma = []
        q1_done = []
        kv1_done = []
        for pan in range(4):
            dw = p.add("sync", lambda e, pan=pan: e.dma_start(
                out=wpanb1, in_=qkvT_v[:, :, pan * 512:(pan + 1) * 512]), dma=True,
                deps=wpan_war + [o_ag_qkv])
            for mt in range(4):
                m = pan * 4 + mt
                sel = m % 2
                mm_ps = ps_a[:, sel * TC:(sel + 1) * TC]

                def mm(t, mt=mt, mm_ps=mm_ps):
                    for h in (0, 1):
                        for ft in range(NFT):
                            inst = t.matmul(out=mm_ps[:, h * 512:(h + 1) * 512],
                                            lhsT=wpanb1[:, ft, mt * 128:(mt + 1) * 128],
                                            rhs=lnx[:, ft, h * 512:(h + 1) * 512],
                                            start=(ft == 0), stop=(ft == NFT - 1))
                    return inst
                om = p.add("tensor", mm, deps=[dw] + ln_ops + psum_war[sel])
                oe = p.add("scalar", lambda s, sel=sel, mm_ps=mm_ps: s.activation(
                    out=stg1[:, sel], in_=mm_ps, func=AF.Relu), deps=[om] + out_dma[-2:])
                od = p.add("gpsimd", lambda e, m=m, sel=sel: e.dma_start(
                    out=q1d[m * 128:(m + 1) * 128, :], in_=stg1[:, sel]), deps=[oe],
                    dma=True)
                psum_war[sel] = [oe]
                out_dma.append(od)
                q1_done.append(od)
            wpan_war = [om]
        for wi, (coff, out_dram, func) in enumerate(((2048, k1d, AF.Relu),
                                                     (4096, v1d, None))):
            for ch in range(4):
                dw = p.add("sync", lambda e, ch=ch, coff=coff: e.dma_start(
                    out=wpanb1, in_=qkvT_v[:, :, coff + ch * 512:coff + (ch + 1) * 512]),
                    dma=True, deps=wpan_war)
                for tt in range(8):
                    sel = tt % 2
                    mm_ps = ps_a[:, sel * TC:sel * TC + 512]

                    def mm(t, ch=ch, tt=tt, mm_ps=mm_ps):
                        for ft in range(NFT):
                            inst = t.matmul(out=mm_ps,
                                            lhsT=lnx[:, ft, tt * 128:(tt + 1) * 128],
                                            rhs=wpanb1[:, ft, :],
                                            start=(ft == 0), stop=(ft == NFT - 1))
                        return inst
                    om = p.add("tensor", mm, deps=[dw] + ln_ops + psum_war[sel])
                    if func is None:
                        oe = p.add("vector", lambda v, sel=sel, mm_ps=mm_ps: v.tensor_copy(
                            out=stg1[:, sel, 0:512], in_=mm_ps), deps=[om] + out_dma[-2:])
                    else:
                        oe = p.add("scalar", lambda s, sel=sel, mm_ps=mm_ps: s.activation(
                            out=stg1[:, sel, 0:512], in_=mm_ps, func=AF.Relu),
                            deps=[om] + out_dma[-2:])
                    od = p.add("gpsimd", lambda e, tt=tt, ch=ch, sel=sel,
                               out_dram=out_dram: e.dma_start(
                        out=out_dram[tt * 128:(tt + 1) * 128, ch * 512:(ch + 1) * 512],
                        in_=stg1[:, sel, 0:512]), deps=[oe], dma=True)
                    psum_war[sel] = [oe]
                    out_dma.append(od)
                    kv1_done.append(od)
                wpan_war = [om]
        qkv1_ps_readers = psum_war[0] + psum_war[1]

        p.barrier()
        SC.reset()
        kt_sb = SC.tile("kt_sb", [128, 8, 2048], BF16)
        vt_sb = SC.tile("vt_sb", [128, 8, 2048], BF16)
        vk_sb = SC.tile("vk_sb", [32, 2112], F32)
        d_k = p.add("sync", lambda e: e.dma_start(
            out=kt_sb, in_=k1d.rearrange("(tt q) f -> q tt f", q=128)), dma=True,
            deps=kv1_done[-1:])
        d_v = p.add("sync", lambda e: e.dma_start(
            out=vt_sb, in_=v1d.rearrange("(tt q) f -> q tt f", q=128)), dma=True,
            deps=kv1_done[-1:])
        onescol = B["onescol"]
        ps_war = [qkv1_ps_readers, qkv1_ps_readers]
        evs = []
        for hd in range(64):
            sel = hd % 2
            slot = ps_b[0:32, sel * 512:sel * 512 + 33]

            def mm(t, hd=hd, slot=slot):
                for tt in range(8):
                    t.matmul(out=slot[:, 0:32], lhsT=kt_sb[:, tt, hd * 32:(hd + 1) * 32],
                             rhs=vt_sb[:, tt, hd * 32:(hd + 1) * 32],
                             start=(tt == 0), stop=(tt == 7))
                for tt in range(8):
                    inst = t.matmul(out=slot[:, 32:33],
                                    lhsT=kt_sb[:, tt, hd * 32:(hd + 1) * 32],
                                    rhs=onescol, start=(tt == 0), stop=(tt == 7))
                return inst
            om = p.add("tensor", mm, deps=[d_k, d_v] + consts + ps_war[sel])
            oe = p.add("vector", lambda v, hd=hd, slot=slot: v.tensor_copy(
                out=vk_sb[:, hd * 33:(hd + 1) * 33], in_=slot), deps=[om])
            ps_war[sel] = [oe]
            evs.append(oe)
        d_vk = p.add("sync", lambda e: e.dma_start(out=vkp, in_=vk_sb), deps=evs[-1:] + evs[-2:-1], dma=True)
        o_ar_vk = p.add("gpsimd", lambda g: g.collective_compute(
            "AllReduce", ALU.add, replica_groups=RG_PAIR,
            ins=[vkp.opt()], outs=[vkr.opt()]), deps=[d_vk], cc=True)
        if UPTO == "G":
            finish([o_ar_vk], xdep)
            return nc

        p.barrier()
        SC.reset()
        q_sb1 = SC.tile("q_sb1", [128, NFT, TC], BF16)
        bd_f = SC.tile("bd_f", [128, NFT, 128], F32)
        bd_b = SC.tile("bd_b", [128, NFT, 128], BF16)
        den_f = SC.tile("den_f", [128, NFT, 4], F32)
        den_b = SC.tile("den_b", [128, NFT, 4], BF16)
        den_sb = SC.tile("den_sb", [4, TC], F32)
        rcp_sb = SC.tile("rcp_sb", [4, TC], F32)
        rcp_bf = SC.tile("rcp_bf", [4, TC], BF16)
        num_sb = SC.tile("num_sb", [128, TC], BF16)
        wpanb2 = SC.tile("wpanb2", [128, NFT, 256], BF16)
        btmp_f2 = SC.tile("btmp_f2", [1, 512], F32)
        brow_b2 = SC.tile("brow_b2", [1, D], BF16)
        ctx_sb = B["lnx"]
        xt = B["xt"]
        onesrow_b = B["onesrow_b"]

        d_q1 = p.add("sync", lambda e: e.dma_start(
            out=q_sb1, in_=q1d.rearrange("(ft q) t -> q ft t", q=128)), dma=True,
            deps=q1_done[-1:])
        o_z1 = p.add("vector", lambda v: v.memset(bd_f, 0.0), se=True)
        o_z2 = p.add("vector", lambda v: v.memset(den_f, 0.0), se=True)
        bd_dmas = []
        for g in range(16):
            for j in range(4):
                h = g * 4 + j
                bd_dmas.append(p.add("sync", lambda e, g=g, j=j, h=h: e.dma_start(
                    out=bd_f[j * 32:(j + 1) * 32, g, j * 32:(j + 1) * 32],
                    in_=vkr[0:32, h * 33:h * 33 + 32]), dma=True,
                    deps=[o_ar_vk, o_z1]))
                bd_dmas.append(p.add("sync", lambda e, g=g, j=j, h=h: e.dma_start(
                    out=den_f[j * 32:(j + 1) * 32, g, j:j + 1],
                    in_=vkr[0:32, h * 33 + 32:h * 33 + 33],
                    allow_slow_non_contiguous=True), dma=True,
                    deps=[o_ar_vk, o_z2]))
        c_bd = p.add("vector", lambda v: v.tensor_copy(out=bd_b, in_=bd_f), deps=bd_dmas)
        c_dn = p.add("vector", lambda v: v.tensor_copy(out=den_b, in_=den_f), deps=bd_dmas)
        c_bp = emit_bias_cast(p, btmp_f2, brow_b2, bproj, D)

        ps_num = ps_a[:, 0:TC]
        ps_dbc = ps_a[:, TC:2 * TC]
        ps_den = ps_b[0:4, 0:TC]
        num_war = ps_war[0]
        dbc_war = ps_war[1]
        den_war = ps_war[0] + ps_war[1]
        ctx_ops = []
        ln_war = list(ln_ops)
        for g in range(NFT):
            o_nm = p.add("tensor", lambda t, g=g: [
                t.matmul(out=ps_num[:, h * 512:(h + 1) * 512], lhsT=bd_b[:, g, :],
                         rhs=q_sb1[:, g, h * 512:(h + 1) * 512], start=True, stop=True)
                for h in (0, 1)][-1], deps=[c_bd, d_q1] + num_war)
            o_dm = p.add("tensor", lambda t, g=g: [
                t.matmul(out=ps_den[:, h * 512:(h + 1) * 512], lhsT=den_b[:, g, :],
                         rhs=q_sb1[:, g, h * 512:(h + 1) * 512], start=True, stop=True)
                for h in (0, 1)][-1], deps=[c_dn, d_q1] + den_war)
            o_ep = p.add("vector", lambda v: v.tensor_single_scalar(
                out=den_sb, in_=ps_den, scalar=EPS_A1, op=ALU.add), deps=[o_dm], se=True)
            o_rc = p.add("vector", lambda v: v.reciprocal(out=rcp_sb, in_=den_sb),
                         deps=[o_ep], se=True)
            o_rb = p.add("vector", lambda v: v.tensor_copy(out=rcp_bf, in_=rcp_sb),
                         deps=[o_rc], se=True)
            den_war = [o_ep]
            o_db = p.add("tensor", lambda t: [
                t.matmul(out=ps_dbc[:, h * 512:(h + 1) * 512], lhsT=bonest,
                         rhs=rcp_bf[:, h * 512:(h + 1) * 512], start=True, stop=True)
                for h in (0, 1)][-1], deps=[o_rb, d_bn] + dbc_war)
            o_ne = p.add("vector", lambda v: v.tensor_copy(out=num_sb, in_=ps_num),
                         deps=[o_nm])
            num_war = [o_ne]
            o_cx = p.add("vector", lambda v, g=g: v.tensor_tensor(
                out=ctx_sb[:, g], in0=num_sb, in1=ps_dbc, op=ALU.mult),
                deps=[o_ne, o_db] + ln_war)
            ln_war = []
            dbc_war = [o_cx]
            ctx_ops.append(o_cx)
        projT_v = projT.rearrange("(ft q) m -> q ft m", q=128)
        psum_war = {0: num_war + [ctx_ops[-1]], 1: dbc_war + [ctx_ops[-1]]}
        wpan_war = []
        x_done2 = {}
        for pan in range(8):
            dw = p.add("sync", lambda e, pan=pan: e.dma_start(
                out=wpanb2, in_=projT_v[:, :, pan * 256:(pan + 1) * 256]), dma=True,
                deps=wpan_war + [o_ag_proj])
            for mt in range(2):
                m = pan * 2 + mt
                sel = m % 2
                mm_ps = ps_a[:, sel * TC:(sel + 1) * TC]

                def mm(t, mt=mt, m=m, mm_ps=mm_ps):
                    for h in (0, 1):
                        for ft in range(NFT):
                            t.matmul(out=mm_ps[:, h * 512:(h + 1) * 512],
                                     lhsT=wpanb2[:, ft, mt * 128:(mt + 1) * 128],
                                     rhs=ctx_sb[:, ft, h * 512:(h + 1) * 512],
                                     start=(ft == 0), stop=False)
                        inst = t.matmul(out=mm_ps[:, h * 512:(h + 1) * 512],
                                        lhsT=brow_b2[:, m * 128:(m + 1) * 128],
                                        rhs=onesrow_b[:, h * 512:(h + 1) * 512],
                                        start=False, stop=True)
                    return inst
                om = p.add("tensor", mm,
                           deps=[dw, c_bp] + ctx_ops[-1:] + consts + psum_war[sel])
                oa = p.add("vector", lambda v, m=m, mm_ps=mm_ps: v.tensor_tensor(
                    out=xt[:, m], in0=xt[:, m], in1=mm_ps, op=ALU.add),
                    deps=[om] + xdep[m])
                psum_war[sel] = [oa]
                x_done2[m] = [oa]
            wpan_war = [om]

        dump_x("dbg_a1", x_done2)
        if UPTO == "H":
            finish([], x_done2)
            return nc

        p.barrier()
        xdep, psr, out_sts = emit_ffn1(x_done2, psum_war[0] + psum_war[1],
                                       store_out=True)
        p.build(stack)
    return nc



_IDENT = np.eye(128, dtype=BF)
_TRIL = np.tril(np.ones((128, 128), np.float32)).astype(BF)
_ONES128 = np.ones((128, 128), BF)
_BONES4 = np.zeros((4, 128), np.float32)
for _j in range(4):
    _BONES4[_j, _j * 32:(_j + 1) * 32] = 1.0
_BONES4 = _BONES4.astype(BF)

_NC_CACHE = {}


def kernel(**inputs):
    inputs = {k: np.asarray(v) for k, v in inputs.items()}
    x = np.asarray(inputs["L0_attn_x"], dtype=np.float32)
    x_fm = np.ascontiguousarray(x.reshape(8192, 2048).T)
    if "mega" not in _NC_CACHE:
        _NC_CACHE["mega"] = build_mega()
    nc = _NC_CACHE["mega"]

    gp = inputs["L0_ffn_gate_W_packed"]; gs = inputs["L0_ffn_gate_scales"]
    up = inputs["L0_ffn_up_W_packed"]; us = inputs["L0_ffn_up_scales"]
    dp = inputs["L0_ffn_down_W_packed"]; ds = inputs["L0_ffn_down_scales"]
    in_maps = []
    for c in range(NCORE):
        odd = c % 2 == 1
        in_maps.append({
            "x_in": np.ascontiguousarray(x_fm[:, c * TC:(c + 1) * TC]),
            "wq": inputs["L0_attn_Wq"], "wk": inputs["L0_attn_Wk"],
            "wv": inputs["L0_attn_Wv"], "wo": inputs["L0_attn_Wo"],
            "bq": inputs["L0_attn_bq"].reshape(1, D),
            "bk": inputs["L0_attn_bk"].reshape(1, D),
            "bv": inputs["L0_attn_bv"].reshape(1, D),
            "bo": inputs["L0_attn_bo"].reshape(1, D),
            "g_pk": np.ascontiguousarray(gp[:, c * 128:(c + 1) * 128]),
            "g_sc": np.ascontiguousarray(gs[:, c * 8:(c + 1) * 8]),
            "u_pk": np.ascontiguousarray(up[:, c * 128:(c + 1) * 128]),
            "u_sc": np.ascontiguousarray(us[:, c * 8:(c + 1) * 8]),
            "d_pk": np.ascontiguousarray(dp[:, c * 1024:(c + 1) * 1024]),
            "d_sc": np.ascontiguousarray(ds[:, c * 64:(c + 1) * 64]),
            "w1_s": np.ascontiguousarray(inputs["L1_ffn_W1"][:, c * 256:(c + 1) * 256]),
            "w2_s": np.ascontiguousarray(inputs["L1_ffn_W2"][:, c * 512:(c + 1) * 512]),
            "w3_s": np.ascontiguousarray(inputs["L1_ffn_W3"][:, c * 512:(c + 1) * 512]),
            "qkv_s": np.ascontiguousarray(inputs["L1_attn_w_qkv"][:, c * 256:(c + 1) * 256]),
            "proj_s": np.ascontiguousarray(inputs["L1_attn_w_proj"][:, c * 256:(c + 1) * 256]),
            "b1": inputs["L1_ffn_b1"].reshape(1, 4096),
            "b2": inputs["L1_ffn_b2"].reshape(1, 4096),
            "b3": inputs["L1_ffn_b3"].reshape(1, D),
            "bproj": inputs["L1_attn_b_proj"].reshape(1, D),
            "ident": _IDENT,
            "maskA": _ONES128 if odd else _TRIL,
            "maskB": _TRIL if odd else _ONES128,
            "maskC": (np.ones if odd else np.zeros)((128, 1024), np.float32).astype(BF),
            "bones": _BONES4,
        })
    res = run_bass_kernel_spmd(nc, in_maps, core_ids=list(range(NCORE)))
    slabs = [res.results[c]["x_out"] for c in range(NCORE)]
    x_fm_out = np.concatenate(slabs, axis=1)
    return np.ascontiguousarray(x_fm_out.T).reshape(4, 2048, 2048).astype(np.float32)


# version 08 (final)
# speedup vs baseline: 1.0103x; 1.0103x over previous
import numpy as np
import ml_dtypes
from contextlib import ExitStack

import concourse.bass as bass
import concourse.mybir as mybir
from concourse.bass_utils import run_bass_kernel_spmd

F32 = mybir.dt.float32
BF16 = mybir.dt.bfloat16
I32 = mybir.dt.int32
U16 = mybir.dt.uint16
AF = mybir.ActivationFunctionType
ALU = mybir.AluOpType
AX = mybir.AxisListType
BF = ml_dtypes.bfloat16

NCORE = 8
DEBUG = False
NO_CC = False
SERIAL_DMA = True
CC_LIMIT = 99
DQG = 16
DQMODE = "full"
ATTN_NH = 16
ATTN_PARTS = "all"
NG3 = 8
A3CHAIN = True
A3DEP = True
DECL = 0xFFFF
EXT_NAMES = set()
STAGE_DMA_Q = "scalar"
UPTO = "Z"
D = 2048
TC = 1024
NFT = D // 128
EPS_LN = 1e-5
EPS_A1 = 1e-15
ATT_SC = 1.0 / float(np.sqrt(128.0))

GELU_ADD = 1.0 / 0.044715
GELU_SCALE = 2.0 * 0.7978845608028654 * 0.044715


class Prog:
    ENGINES = ("sync", "vector", "scalar", "tensor", "gpsimd")

    def __init__(self, nc):
        self.nc = nc
        self.ops = []
        self._bar = []
        self._cc_ops = []
        self._last_dma = {}

    def add(self, eng, fn, deps=(), dma=False, se=False, nobar=False, cc=False):
        deps = [d for d in deps if d is not None]
        if not nobar:
            deps = deps + self._bar
        if dma and SERIAL_DMA and eng in self._last_dma:
            deps = deps + [self._last_dma[eng]]
        self.ops.append((eng, fn, deps, dma, se or cc))
        if dma:
            self._last_dma[eng] = len(self.ops) - 1
        if cc:
            self._cc_ops.append(len(self.ops) - 1)
        return len(self.ops) - 1

    def barrier(self):
        last = {}
        for i, (eng, *_ ) in enumerate(self.ops):
            last[eng] = i
        self._bar = list(last.values())

    def build(self, stack):
        nc = self.nc
        sems = {}
        for e in self.ENGINES:
            sems[(e, "c")] = stack.enter_context(nc.semaphore("semc_" + e))
            sems[(e, "d")] = stack.enter_context(nc.semaphore("semd_" + e))
        SELF_SYNC = ("vector", "scalar", "gpsimd")
        needed = set()
        for eng, fn, deps, dma, se in self.ops:
            for d in deps:
                peng, _, _, pdma, pse = self.ops[d]
                if peng != eng or pse or pdma or peng in SELF_SYNC:
                    needed.add(d)
        last_of = {}
        for i, (eng, fn, deps, dma, se) in enumerate(self.ops):
            last_of[(eng, dma)] = i
            if dma:
                needed.add(i)
        needed.update(self._cc_ops)
        needed.update(last_of.values())
        val = {}
        ctr = {(e, k): 0 for e in self.ENGINES for k in ("c", "d")}
        kind = {}
        for i, (eng, fn, deps, dma, se) in enumerate(self.ops):
            kind[i] = (eng, "d" if dma else "c")
            if i in needed:
                ctr[kind[i]] += 16 if dma else 1
                val[i] = ctr[kind[i]]
        block = stack.enter_context(nc.Block())

        def make_body(ename):
            def body(eng):
                waited = {}
                for i, (oeng, fn, deps, dma, se) in enumerate(self.ops):
                    if oeng != ename:
                        continue
                    for d in deps:
                        peng, _, _, pdma, pse = self.ops[d]
                        if peng == ename and not (pse or pdma) and \
                                ename not in SELF_SYNC:
                            continue
                        v = val.get(d)
                        k = kind[d]
                        if v is None or waited.get(k, 0) >= v:
                            continue
                        eng.wait_ge(sems[k], v)
                        waited[k] = v
                    inst = fn(eng)
                    if i in needed:
                        inst.then_inc(sems[kind[i]], 16 if dma else 1)
                if ename == "gpsimd":
                    for k2, c2 in ctr.items():
                        if c2:
                            eng.wait_ge(sems[k2], c2)
            return body

        for e in self.ENGINES:
            getattr(block, e)(make_body(e))
        return sems, ctr


class Scratch:

    def __init__(self, nc, backing_handle):
        self.nc = nc
        self.backing = backing_handle
        self.base = nc.lookup_mloc(backing_handle).addr
        self.cap = backing_handle.shape[1]
        self.off = 0
        self.n = 0

    def reset(self, base=0):
        self.off = base

    def tile(self, name, shape, dtype):
        per_part = int(np.prod(shape[1:])) * mybir.dt.size(dtype)
        off = (self.off + 31) & ~31
        assert off + per_part <= self.cap, (
            f"scratch overflow: {name} needs {per_part}B at {off}, cap {self.cap}")
        self.off = off + per_part
        self.n += 1
        nm = f"sc{self.n}_{name}"
        hnd = self.nc.alloc_sbuf_tensor_at(nm, shape, dtype, offset=self.base + off)
        return hnd[:]



def emit_ln(p, B, x_dep, extra_rdeps=(), psum_war=()):
    nc = p.nc
    xt, lnx = B["xt"], B["lnx"]
    xb, x2b = B["xb"], B["x2b"]
    onescol = B["onescol"]
    onesrow_f = B["onesrow_f"]
    srow = B["srow"]
    rstd = B["rstd"]
    ps_a = B["ps_a"]
    ps_b = B["ps_b"]
    sum_ps = ps_b[0:1, 0:TC]
    sq_ps = ps_b[32:33, 0:TC]
    rstd_b = ps_a[:, 0:TC]
    mr_b = ps_a[:, TC:2 * TC]
    m_row = srow[:, 0:TC]
    q_row = srow[:, TC:2 * TC]
    var_row = srow[:, 2 * TC:3 * TC]
    mr_row = srow[:, 3 * TC:4 * TC]

    cast_ops = [None, None]
    sq_ops = [None, None]
    mm_s_ops = [None, None]
    mm_q_ops = [None, None]
    o_sum = o_sq = None
    first_deps = list(psum_war)
    for ft in range(NFT):
        r = ft % 2
        c = p.add("vector", lambda v, ft=ft, r=r: v.tensor_copy(out=xb[:, r], in_=xt[:, ft]),
                  deps=list(x_dep[ft]) + [mm_s_ops[r]])
        s = p.add("scalar", lambda s, ft=ft, r=r: s.activation(out=x2b[:, r], in_=xt[:, ft],
                                                               func=AF.Square),
                  deps=list(x_dep[ft]) + [mm_q_ops[r]])

        def _mm_half(t, out_ps, rhs, ft):
            for h in (0, 1):
                inst = t.matmul(out=out_ps[:, h * 512:(h + 1) * 512], lhsT=onescol,
                                rhs=rhs[:, h * 512:(h + 1) * 512],
                                start=(ft == 0), stop=(ft == NFT - 1))
            return inst
        o_sum = p.add("tensor", lambda t, ft=ft, r=r: _mm_half(t, sum_ps, xb[:, r], ft),
                      deps=[c] + list(extra_rdeps) + (first_deps if ft == 0 else []))
        o_sq = p.add("tensor", lambda t, ft=ft, r=r: _mm_half(t, sq_ps, x2b[:, r], ft),
                     deps=[s] + list(extra_rdeps) + (first_deps if ft == 0 else []))
        cast_ops[r] = c
        sq_ops[r] = s
        mm_s_ops[r] = o_sum
        mm_q_ops[r] = o_sq

    o_m = p.add("vector", lambda v: v.tensor_single_scalar(out=m_row, in_=sum_ps,
                                                           scalar=1.0 / D, op=ALU.mult),
                deps=[o_sum], se=True)
    o_q = p.add("vector", lambda v: v.tensor_single_scalar(out=q_row, in_=sq_ps,
                                                           scalar=1.0 / D, op=ALU.mult),
                deps=[o_sq], se=True)
    o_m2 = p.add("vector", lambda v: v.scalar_tensor_tensor(out=var_row, in0=m_row, scalar=-1.0,
                                                            in1=m_row, op0=ALU.mult, op1=ALU.mult),
                 deps=[o_m], se=True)
    o_var = p.add("vector", lambda v: v.scalar_tensor_tensor(
        out=var_row, in0=q_row, scalar=EPS_LN, in1=var_row, op0=ALU.add, op1=ALU.add),
        deps=[o_q, o_m2], se=True)
    o_ln = p.add("scalar", lambda s: s.activation(out=var_row, in_=var_row, func=AF.Ln),
                 deps=[o_var], se=True)
    o_rstd = p.add("scalar", lambda s: s.activation(out=rstd, in_=var_row, func=AF.Exp,
                                                    scale=-0.5),
                   deps=[o_ln], se=True)
    o_mr = p.add("vector", lambda v: v.tensor_tensor(out=mr_row, in0=m_row, in1=rstd,
                                                     op=ALU.mult),
                 deps=[o_m, o_rstd], se=True)

    def _bcast(t, out_ps, row):
        for h in (0, 1):
            inst = t.matmul(out=out_ps[:, h * 512:(h + 1) * 512], lhsT=onesrow_f,
                            rhs=row[:, h * 512:(h + 1) * 512], start=True, stop=True)
        return inst
    o_rb = p.add("tensor", lambda t: _bcast(t, rstd_b, rstd),
                 deps=[o_rstd] + list(extra_rdeps) + list(psum_war))
    o_mb = p.add("tensor", lambda t: _bcast(t, mr_b, mr_row),
                 deps=[o_mr] + list(extra_rdeps) + list(psum_war))
    ln_ops = []
    tmp = B["lntmp"]
    for ft in range(NFT):
        o_t1 = p.add("vector", lambda v, ft=ft: v.tensor_tensor(out=tmp, in0=xt[:, ft],
                                                                in1=rstd_b, op=ALU.mult),
                     deps=[o_rb] + list(x_dep[ft]))
        o_t2 = p.add("vector", lambda v, ft=ft: v.tensor_tensor(out=lnx[:, ft], in0=tmp,
                                                                in1=mr_b, op=ALU.subtract),
                     deps=[o_mb, o_t1])
        ln_ops.append(o_t2)
    return ln_ops, ln_ops + [o_m, o_q]


def emit_bias_cast(p, tmp_f, dst_b, b_dram, width, scale=None, after=()):
    last = None
    prev_cast = list(after)
    for ch in range(width // 512):
        d = p.add("sync", lambda e, ch=ch: e.dma_start(
            out=tmp_f, in_=b_dram[:, ch * 512:(ch + 1) * 512]), dma=True, deps=prev_cast)
        if scale is None:
            last = p.add("vector", lambda v, ch=ch: v.tensor_copy(
                out=dst_b[:, ch * 512:(ch + 1) * 512], in_=tmp_f), deps=[d], se=True)
        else:
            last = p.add("vector", lambda v, ch=ch: v.tensor_single_scalar(
                out=dst_b[:, ch * 512:(ch + 1) * 512], in_=tmp_f, scalar=scale,
                op=ALU.mult), deps=[d], se=True)
        prev_cast = [last]
    return last



def build_mega():
    nc = bass.Bass("TRN2", target_bir_lowering=False, debug=False)
    EXT_NAMES.clear()
    EXT_NAMES.update(("x_in", "g_pk", "g_sc"))

    def ext_in(name, shape, dtype):
        EXT_NAMES.add(name)
        return nc.dram_tensor(name, shape, dtype, kind="ExternalInput").ap()

    def internal(name, shape, dtype):
        return nc.dram_tensor(name, shape, dtype, kind="Internal").ap()

    x_in = nc.dram_tensor("x_in", [D, TC], F32, kind="ExternalInput").ap()
    g_pk = nc.dram_tensor("g_pk", [16384, 128], I32, kind="ExternalInput").ap()
    g_sc = nc.dram_tensor("g_sc", [16384, 8], F32, kind="ExternalInput").ap()
    if DECL & 1:
        wq = ext_in("wq", [D, D], F32)
        wk = ext_in("wk", [D, D], F32)
        wv = ext_in("wv", [D, D], F32)
        wo = ext_in("wo", [D, D], F32)
        bq = ext_in("bq", [1, D], F32)
        bk = ext_in("bk", [1, D], F32)
        bv = ext_in("bv", [1, D], F32)
        bo = ext_in("bo", [1, D], F32)
    if DECL & 2:
        u_pk = ext_in("u_pk", [16384, 128], I32)
        u_sc = ext_in("u_sc", [16384, 8], F32)
        d_pk = ext_in("d_pk", [2048, 1024], I32)
        d_sc = ext_in("d_sc", [2048, 64], F32)
    if DECL & 4:
        w1_s = ext_in("w1_s", [4096, 256], F32)
        w2_s = ext_in("w2_s", [4096, 512], F32)
        w3_s = ext_in("w3_s", [2048, 512], F32)
        qkv_s = ext_in("qkv_s", [6144, 256], F32)
        proj_s = ext_in("proj_s", [2048, 256], F32)
        b1 = ext_in("b1", [1, 4096], F32)
        b2 = ext_in("b2", [1, 4096], F32)
        b3 = ext_in("b3", [1, D], F32)
        bproj = ext_in("bproj", [1, D], F32)
    if DECL & 8:
        ident = ext_in("ident", [128, 128], BF16)
        maskA_in = ext_in("maskA", [128, 128], BF16)
        maskB_in = ext_in("maskB", [128, 128], BF16)
        maskC_in = ext_in("maskC", [128, 1024], BF16)
        bones = ext_in("bones", [4, 128], BF16)
    x_out = nc.dram_tensor("x_out", [D, TC], F32, kind="ExternalOutput").ap()

    gT_s = internal("gT_s", [256, 16384], BF16)
    gT = internal("gT", [D, 16384], BF16)
    uT = internal("uT", [D, 16384], BF16)
    if DECL & 16:
        uT_s = internal("uT_s", [256, 16384], BF16)
        dT_s = internal("dT_s", [2048, 2048], BF16)
        dT = internal("dT", [16384, D], BF16)
        w1T_s = internal("w1T_s", [256, 4096], BF16)
        w2T_s = internal("w2T_s", [512, 4096], BF16)
        w3T_s = internal("w3T_s", [512, 2048], BF16)
        qkvT_s = internal("qkvT_s", [256, 6144], BF16)
        projT_s = internal("projT_s", [256, 2048], BF16)
        w1T = internal("w1T", [D, 4096], BF16)
        w2T = internal("w2T", [4096, 4096], BF16)
        w3T = internal("w3T", [4096, D], BF16)
        qkvT = internal("qkvT", [D, 6144], BF16)
        projT = internal("projT", [D, D], BF16)
        q0d = internal("q0d", [D, TC], BF16)
        k0d = internal("k0d", [D, TC], BF16)
        v0d = internal("v0d", [D, TC], BF16)
        k0g = internal("k0g", [2 * D, TC], BF16)
        v0g = internal("v0g", [2 * D, TC], BF16)
        q1d = internal("q1d", [D, TC], BF16)
        k1d = internal("k1d", [TC, D], BF16)
        v1d = internal("v1d", [TC, D], BF16)
        vkp = internal("vkp", [32, 2112], F32)
        vkr = internal("vkr", [32, 2112], F32)

    DBG = {}
    if DEBUG:
        EXT_NAMES.add("dbg_dts"); EXT_NAMES.add("dbg_dtg")
        DBG["dbg_dts"] = nc.dram_tensor("dbg_dts", [2048, 2048], BF16,
                                        kind="ExternalOutput").ap()
        DBG["dbg_dtg"] = nc.dram_tensor("dbg_dtg", [2048, 2048], BF16,
                                        kind="ExternalOutput").ap()
    if DEBUG and (DECL & 32):
        for tag in ("dbg_f0a", "dbg_p0", "dbg_f0b", "dbg_f1a", "dbg_a1"):
            EXT_NAMES.add(tag)
            DBG[tag] = nc.dram_tensor(tag, [D, TC], F32, kind="ExternalOutput").ap()
        EXT_NAMES.add("dbg_ctx")
        DBG["dbg_ctx"] = nc.dram_tensor("dbg_ctx", [D, TC], BF16,
                                        kind="ExternalOutput").ap()

    RG_ALL = [list(range(NCORE))]
    RG_PAIR = [[0, 1], [2, 3], [4, 5], [6, 7]]

    with ExitStack() as stack:
        def sb(name, shape, dtype):
            return stack.enter_context(nc.sbuf_tensor(name, shape, dtype))[:]

        B = {}
        if not (DECL & 64):
            B["xt"] = sb("xt", [128, 16, 8], F32)
        else:
            B["xt"] = sb("xt", [128, NFT, TC], F32)
        if DECL & 64:
            B["lnx"] = sb("lnx", [128, NFT, TC], BF16)
            B["xb"] = sb("xb_", [128, 2, TC], BF16)
            B["x2b"] = sb("x2b", [128, 2, TC], BF16)
            B["onescol"] = sb("onescol", [128, 1], BF16)
            B["onesrow_f"] = sb("onesrow_f", [1, 128], F32)
            B["onesrow_b"] = sb("onesrow_b", [1, TC], BF16)
            B["srow"] = sb("srow", [1, 4 * TC], F32)
            B["rstd"] = sb("rstd", [1, TC], F32)
            B["lntmp"] = sb("lntmp", [128, TC], F32)
            identt = sb("identt", [128, 128], BF16)
            maskA = sb("maskA_t", [128, 128], BF16)
            maskB = sb("maskB_t", [128, 128], BF16)
            maskC = sb("maskC_t", [128, 1024], BF16)
            bonest = sb("bonest", [4, 128], BF16)

        scratch_back = stack.enter_context(nc.sbuf_tensor("scratch", [128, 74 * 1024],
                                                          mybir.dt.uint8))
        SC = Scratch(nc, scratch_back)

        if DECL & 128:
            B["ps_a"] = stack.enter_context(nc.psum_tensor("ps_a", [128, 2048], F32))[:]
            B["ps_b"] = stack.enter_context(nc.psum_tensor("ps_b", [128, 1024], F32))[:]
            ps_t = stack.enter_context(nc.psum_tensor("ps_t", [128, 2048], BF16))[:]
            ps_a, ps_b = B["ps_a"], B["ps_b"]

        p = Prog(nc)

        if UPTO == "A3":
            pk_v = g_pk.rearrange("(g q) w -> q g w", q=128)
            pkt3 = SC.tile("pkt3", [128, NG3, 128], I32)
            prev = []
            for g in range(DQG):
                d1 = p.add("sync", lambda e, g=g: e.dma_start(
                    out=pkt3, in_=pk_v[:, g * NG3:(g + 1) * NG3, :]), dma=True,
                    deps=prev if A3CHAIN else [])
                prev = [d1]
            o1 = p.add("gpsimd", lambda g: g.collective_compute(
                "AllGather", ALU.bypass, replica_groups=RG_ALL,
                ins=[gT_s.opt()], outs=[gT.opt()]), deps=prev if A3DEP else [],
                cc=True)
            o2 = p.add("gpsimd", lambda g: g.collective_compute(
                "AllGather", ALU.bypass, replica_groups=RG_ALL,
                ins=[gT_s.opt()], outs=[uT.opt()]), deps=[o1], cc=True)
            p.add("gpsimd", lambda e: e.dma_start(
                out=x_out[0:128, :],
                in_=pkt3.bitcast(F32).rearrange("p a w -> p (a w)")[:, 0:1024]),
                deps=[o1, o2] + prev, dma=True)
            p.build(stack)
            return nc

        consts = [
            p.add("vector", lambda v: v.memset(B["onescol"], 1.0), se=True),
            p.add("vector", lambda v: v.memset(B["onesrow_f"], 1.0), se=True),
            p.add("vector", lambda v: v.memset(B["onesrow_b"], 1.0), se=True),
        ]
        d_id = p.add("sync", lambda e: e.dma_start(out=identt, in_=ident), dma=True)
        d_mA = p.add("sync", lambda e: e.dma_start(out=maskA, in_=maskA_in), dma=True)
        d_mB = p.add("sync", lambda e: e.dma_start(out=maskB, in_=maskB_in), dma=True)
        d_mC = p.add("sync", lambda e: e.dma_start(out=maskC, in_=maskC_in), dma=True)
        d_bn = p.add("sync", lambda e: e.dma_start(out=bonest, in_=bones), dma=True)
        x_ld = p.add("sync", lambda e: e.dma_start(
            out=B["xt"], in_=x_in.rearrange("(ft p) t -> p ft t", p=128)), dma=True)
        x_dep = {ft: [x_ld] for ft in range(NFT)}

        SC.reset()
        NG = 8
        pkt = SC.tile("pkt", [128, NG, 128], I32)
        sct = SC.tile("sct", [128, NG, 8], F32)
        t_lo = SC.tile("t_lo", [128, NG * 128], I32)
        t_hi = SC.tile("t_hi", [128, NG * 128], I32)
        t_u = SC.tile("t_u", [128, NG * 128], I32)
        t_sg = SC.tile("t_sg", [128, NG * 128], I32)
        t_a = SC.tile("t_a", [128, NG * 128], I32)
        t_b = SC.tile("t_b", [128, NG * 128], I32)
        u16t = SC.tile("u16t", [128, NG * 256], U16)
        dqb = SC.tile("dqb", [128, NG, 256], BF16)
        stgu = SC.tile("stgu", [128, 2, 2048], BF16)
        stg_gu = stgu.rearrange("p s (a n) -> p s a n", a=2)
        stg_dn = stgu.rearrange("p s (a n) -> p s a n", a=16)

        def dequant_gu(pk_dram, sc_dram, out_dram, after=()):
            pk_v = pk_dram.rearrange("(g q) w -> q g w", q=128)
            sc_v = sc_dram.rearrange("(g q) s -> q g s", q=128)
            out_v = out_dram.rearrange("(a q) n -> q a n", q=128)
            dma_dep = list(after)
            st_ops = []
            last_ev = None
            stg_war = [None, None]
            for g in range(DQG):
                sel = g % 2
                d1 = p.add("sync", lambda e, g=g: e.dma_start(
                    out=pkt, in_=pk_v[:, g * NG:(g + 1) * NG, :]), dma=True,
                    deps=dma_dep[-1:])
                d2 = p.add("sync", lambda e, g=g: e.dma_start(
                    out=sct, in_=sc_v[:, g * NG:(g + 1) * NG, :]), dma=True,
                    deps=dma_dep[-1:])
                if DQMODE == "dma":
                    dma_dep = [d1, d2]
                    st_ops.append(d1)
                    continue
                pkf = pkt.rearrange("p a w -> p (a w)")
                o1 = p.add("vector", lambda v: v.tensor_single_scalar(
                    out=t_lo, in_=pkf, scalar=15, op=ALU.bitwise_and), deps=[d1])
                o2 = p.add("vector", lambda v: v.tensor_single_scalar(
                    out=t_hi, in_=pkf, scalar=4, op=ALU.logical_shift_right), deps=[d1])
                uops = []
                for i, plane in enumerate((t_lo, t_hi)):
                    pdep = o1 if i == 0 else o2
                    q1 = p.add("vector", lambda v, plane=plane: v.tensor_single_scalar(
                        out=t_u, in_=plane, scalar=7, op=ALU.bitwise_and), deps=[pdep])
                    q2 = p.add("vector", lambda v, plane=plane: v.tensor_single_scalar(
                        out=t_sg, in_=plane, scalar=8, op=ALU.bitwise_and), deps=[pdep])
                    q3 = p.add("vector", lambda v: v.tensor_scalar(
                        out=t_a, in0=t_u, scalar1=64, scalar2=0x3F00,
                        op0=ALU.mult, op1=ALU.add), deps=[q1])
                    q4 = p.add("vector", lambda v: v.tensor_single_scalar(
                        out=t_b, in_=t_u, scalar=0x3F00, op=ALU.mult), deps=[q1])
                    q5 = p.add("vector", lambda v: v.tensor_tensor(
                        out=t_a, in0=t_a, in1=t_b, op=ALU.min), deps=[q3, q4])
                    q6 = p.add("vector", lambda v: v.scalar_tensor_tensor(
                        out=t_a, in0=t_sg, scalar=4096, in1=t_a,
                        op0=ALU.mult, op1=ALU.add), deps=[q2, q5], se=True)
                    q7 = p.add("vector", lambda v, i=i: v.tensor_copy(
                        out=u16t.rearrange("p (w two) -> p w two", two=2)[:, :, i],
                        in_=t_a), deps=[q6], se=True)
                    uops.append(q7)
                dqv = dqb.rearrange("p a (s e) -> p a s e", e=32)
                o_sc = p.add("vector", lambda v: v.tensor_tensor(
                    out=dqv, in0=u16t.bitcast(BF16).rearrange("p (a s e) -> p a s e",
                                                              a=NG, e=32),
                    in1=sct.unsqueeze(3).broadcast_to((128, NG, 8, 32)),
                    op=ALU.mult), deps=uops + [d2], se=True)
                dma_dep = [o_sc]
                if DQMODE == "vec":
                    st_ops.append(o_sc)
                    continue
                evs = []
                for a in range(NG):
                    tr1 = p.add("tensor", lambda t, a=a: t.transpose(
                        out=ps_t[:, 0:128], in_=dqb[:, a, 0:128], identity=identt),
                        deps=[o_sc, d_id] + ([last_ev] if last_ev is not None else []))
                    tr2 = p.add("tensor", lambda t, a=a: t.transpose(
                        out=ps_t[:, 128:256], in_=dqb[:, a, 128:256], identity=identt),
                        deps=[tr1])
                    ev = p.add("vector", lambda v, a=a, sel=sel: v.tensor_copy(
                        out=stg_gu[:, sel, 0:2, a * 128:(a + 1) * 128],
                        in_=ps_t[:, 0:256].rearrange("p (a n) -> p a n", a=2)),
                        deps=[tr1, tr2, stg_war[sel]])
                    last_ev = ev
                    evs.append(ev)
                st = p.add(STAGE_DMA_Q, lambda e, g=g, sel=sel: e.dma_start(
                    out=out_v[:, :, g * NG * 128:(g + 1) * NG * 128],
                    in_=stg_gu[:, sel, :, :]), deps=evs, dma=True)
                stg_war[sel] = st
                st_ops.append(st)
            return st_ops

        def dequant_down(after=()):
            pk_v = d_pk.rearrange("(g q) w -> q g w", q=128)
            sc_v = d_sc.rearrange("(g q) s -> q g s", q=128)
            out_v = dT_s.rearrange("(a q) n -> q a n", q=128)
            dma_dep = list(after)
            st_ops = []
            last_ev = None
            stg_war = [None, None]
            W = 1024
            for g in range(16):
                sel = g % 2
                d1 = p.add("sync", lambda e, g=g: e.dma_start(
                    out=pkt[:, 0:8, :].rearrange("p a w -> p (a w)"),
                    in_=pk_v[:, g, :]), dma=True, deps=dma_dep[-1:])
                d2 = p.add("sync", lambda e, g=g: e.dma_start(
                    out=sct[:, 0:8, :].rearrange("p a s -> p (a s)"),
                    in_=sc_v[:, g, :]), dma=True, deps=dma_dep[-1:])
                o_half = p.add("vector", lambda v: v.tensor_single_scalar(
                    out=sct[:, 0:8, :], in_=sct[:, 0:8, :], scalar=0.5, op=ALU.mult),
                    deps=[d2], se=True)
                pkf = pkt[:, 0:8, :].rearrange("p a w -> p (a w)")
                o1 = p.add("vector", lambda v: v.tensor_single_scalar(
                    out=t_lo[:, 0:W], in_=pkf, scalar=15, op=ALU.bitwise_and), deps=[d1])
                o2 = p.add("vector", lambda v: v.tensor_single_scalar(
                    out=t_hi[:, 0:W], in_=pkf, scalar=4, op=ALU.logical_shift_right),
                    deps=[d1])
                uops = []
                for i, plane in enumerate((t_lo, t_hi)):
                    pdep = o1 if i == 0 else o2
                    q1 = p.add("vector", lambda v, plane=plane: v.tensor_single_scalar(
                        out=t_u[:, 0:W], in_=plane[:, 0:W], scalar=7, op=ALU.bitwise_and),
                        deps=[pdep])
                    q2 = p.add("vector", lambda v, plane=plane: v.tensor_single_scalar(
                        out=t_sg[:, 0:W], in_=plane[:, 0:W], scalar=8, op=ALU.bitwise_and),
                        deps=[pdep])
                    q3 = p.add("vector", lambda v: v.tensor_scalar(
                        out=t_a[:, 0:W], in0=t_u[:, 0:W], scalar1=64, scalar2=0x3F00,
                        op0=ALU.mult, op1=ALU.add), deps=[q1])
                    q4 = p.add("vector", lambda v: v.tensor_single_scalar(
                        out=t_b[:, 0:W], in_=t_u[:, 0:W], scalar=0x3F00, op=ALU.mult),
                        deps=[q1])
                    q5 = p.add("vector", lambda v: v.tensor_tensor(
                        out=t_a[:, 0:W], in0=t_a[:, 0:W], in1=t_b[:, 0:W], op=ALU.min),
                        deps=[q3, q4])
                    q6 = p.add("vector", lambda v: v.scalar_tensor_tensor(
                        out=t_a[:, 0:W], in0=t_sg[:, 0:W], scalar=4096, in1=t_a[:, 0:W],
                        op0=ALU.mult, op1=ALU.add), deps=[q2, q5], se=True)
                    q7 = p.add("vector", lambda v, i=i: v.tensor_copy(
                        out=u16t[:, 0:2 * W].rearrange("p (w two) -> p w two", two=2)[:, :, i],
                        in_=t_a[:, 0:W]), deps=[q6], se=True)
                    uops.append(q7)
                o_sc = p.add("vector", lambda v: v.tensor_tensor(
                    out=dqb[:, 0:8, :].rearrange("p a (s e) -> p (a s) e", e=32),
                    in0=u16t[:, 0:2048].bitcast(BF16).rearrange("p (b e) -> p b e", e=32),
                    in1=sct[:, 0:8, :].rearrange("p a s -> p (a s)").unsqueeze(2)
                        .broadcast_to((128, 64, 32)),
                    op=ALU.mult), deps=uops + [o_half], se=True)
                dma_dep = [o_sc]
                evs = []
                dqf = dqb[:, 0:8, :].rearrange("p a w -> p (a w)")
                for a in range(16):
                    tr1 = p.add("tensor", lambda t, a=a: t.transpose(
                        out=ps_t[:, 0:128], in_=dqf[:, a * 128:(a + 1) * 128],
                        identity=identt),
                        deps=[o_sc, d_id] + ([last_ev] if last_ev is not None else []))
                    ev = p.add("vector", lambda v, a=a, sel=sel: v.tensor_copy(
                        out=stg_dn[:, sel, a, :], in_=ps_t[:, 0:128]),
                        deps=[tr1, stg_war[sel]])
                    last_ev = ev
                    evs.append(ev)
                st = p.add(STAGE_DMA_Q, lambda e, g=g, sel=sel: e.dma_start(
                    out=out_v[:, :, g * 128:(g + 1) * 128],
                    in_=stg_dn[:, sel, :, :]), deps=evs, dma=True)
                stg_war[sel] = st
                st_ops.append(st)
            return st_ops

        def transpose_cast(w_dram, O, ncols, out_dram, scale=None, after=(),
                           bufs=None):
            wldb, wcbb, stgb = bufs
            otile = O // 128
            in_v = w_dram.rearrange("(o q) c -> q o c", q=128)
            out_v = out_dram.rearrange("(a q) n -> q a n", q=128)
            d1 = p.add("sync", lambda e: e.dma_start(
                out=wldb[:, 0:otile, 0:ncols], in_=in_v), dma=True, deps=list(after))
            if scale is None:
                oc = p.add("vector", lambda v: v.tensor_copy(
                    out=wcbb[:, 0:otile, 0:ncols], in_=wldb[:, 0:otile, 0:ncols]),
                    deps=[d1])
            else:
                oc = p.add("vector", lambda v: v.tensor_single_scalar(
                    out=wcbb[:, 0:otile, 0:ncols], in_=wldb[:, 0:otile, 0:ncols],
                    scalar=scale, op=ALU.mult), deps=[d1])
            sts = []
            last_ev = None
            stg_war = [None, None]
            for ot in range(otile):
                sel = ot % 2
                evs = []
                for cb in range(ncols // 128):
                    tr = p.add("tensor", lambda t, ot=ot, cb=cb: t.transpose(
                        out=ps_t[:, 0:128], in_=wcbb[:, ot, cb * 128:(cb + 1) * 128],
                        identity=identt),
                        deps=[oc, d_id] + ([last_ev] if last_ev is not None else []))
                    ev = p.add("vector", lambda v, cb=cb, sel=sel: v.tensor_copy(
                        out=stgb[:, sel, cb, 0:128],
                        in_=ps_t[:, 0:128]), deps=[tr, stg_war[sel]])
                    last_ev = ev
                    evs.append(ev)
                st = p.add(STAGE_DMA_Q, lambda e, ot=ot, sel=sel: e.dma_start(
                    out=out_v[:, 0:ncols // 128, ot * 128:(ot + 1) * 128],
                    in_=stgb[:, sel, 0:ncols // 128, 0:128]), deps=evs, dma=True)
                stg_war[sel] = st
                sts.append(st)
            return sts

        g_sts = dequant_gu(g_pk, g_sc, gT_s)
        if UPTO == "A2":
            o1 = p.add("gpsimd", lambda g: g.collective_compute(
                "AllGather", ALU.bypass, replica_groups=RG_ALL,
                ins=[gT_s.opt()], outs=[gT.opt()]), deps=g_sts, cc=True)
            o2 = p.add("gpsimd", lambda g: g.collective_compute(
                "AllGather", ALU.bypass, replica_groups=RG_ALL,
                ins=[gT_s.opt()], outs=[uT.opt()]), deps=g_sts, cc=True)
            st = p.add("gpsimd", lambda e: e.dma_start(
                out=x_out.rearrange("(ft p) t -> p ft t", p=128), in_=B["xt"]),
                deps=[x_ld, o1, o2], dma=True)
            p.build(stack)
            return nc
        u_sts = dequant_gu(u_pk, u_sc, uT_s, after=g_sts[-1:])
        d_sts = dequant_down(after=u_sts[-1:])

        cc_count = [0]

        def ag(ins_ap, outs_ap, rg, deps):
            cc_count[0] += 1
            if NO_CC or cc_count[0] > CC_LIMIT:
                return None
            return p.add("gpsimd", lambda g: g.collective_compute(
                "AllGather", ALU.bypass, replica_groups=rg,
                ins=[ins_ap.opt()], outs=[outs_ap.opt()]), deps=deps, cc=True)

        o_ag_g = ag(gT_s, gT, RG_ALL, g_sts)
        o_ag_u = ag(uT_s, uT, RG_ALL, u_sts)
        o_ag_d = ag(dT_s, dT, RG_ALL, d_sts)
        if DEBUG:
            p.add("scalar", lambda e: e.dma_start(
                out=DBG["dbg_dts"], in_=dT_s), deps=d_sts, dma=True)
            p.add("scalar", lambda e: e.dma_start(
                out=DBG["dbg_dtg"], in_=dT[0:2048, :]), deps=[o_ag_d], dma=True)

        def emit_ffn0(x_dep, prior_psum_readers):
            SC.reset()
            gbuf = SC.tile("gbuf", [128, NFT, 512], BF16)
            ubuf = SC.tile("ubuf", [128, NFT, 512], BF16)
            dbuf = SC.tile("dbuf", [128, 4, 2048], BF16)
            ibuf = SC.tile("ibuf", [128, 4, TC], BF16)
            t_s2 = SC.tile("t_s2", [128, TC], F32)
            t_a = SC.tile("t_a2", [128, TC], F32)
            t_sgl = SC.tile("t_sgl", [128, TC], F32)
            t_p = SC.tile("t_p", [128, TC], F32)
            lnx = B["lnx"]
            xt = B["xt"]

            ln_ops, psa_readers = emit_ln(p, B, x_dep, extra_rdeps=consts,
                                          psum_war=prior_psum_readers)
            gT_v = gT.rearrange("(ft q) n -> q ft n", q=128)
            uT_v = uT.rearrange("(ft q) n -> q ft n", q=128)
            dT_v = dT.rearrange("(nt q) h -> q nt h", q=128)
            gate_ps = ps_a[:, 0:TC]
            up_ps = ps_a[:, TC:2 * TC]
            dslots = [ps_b[:, 0:512], ps_b[:, 512:1024]]
            dslot_war = [psa_readers, psa_readers]

            NCH = 32
            CH = 512
            NSUB = 4
            prev_panel_readers = [o_ag_d]
            last_gate_user = {0: psa_readers, 1: psa_readers}
            x_accum = dict(x_dep)
            ln_dep_all = list(ln_ops)
            dsl = 0
            for c in range(NCH):
                dg = p.add("sync", lambda e, c=c: e.dma_start(
                    out=gbuf, in_=gT_v[:, :, c * CH:(c + 1) * CH]),
                    deps=prev_panel_readers + [o_ag_g], dma=True)
                du = p.add("sync", lambda e, c=c: e.dma_start(
                    out=ubuf, in_=uT_v[:, :, c * CH:(c + 1) * CH]),
                    deps=prev_panel_readers + [o_ag_u], dma=True)
                dd = p.add("sync", lambda e, c=c: e.dma_start(
                    out=dbuf, in_=dT_v[:, c * NSUB:(c + 1) * NSUB, :]),
                    deps=prev_panel_readers + [o_ag_d], dma=True)
                panel_readers = []
                inter_ready = []
                for s in range(NSUB):
                    def mm_gate(t, s=s):
                        for h in (0, 1):
                            for ft in range(NFT):
                                inst = t.matmul(out=gate_ps[:, h * 512:(h + 1) * 512],
                                                lhsT=gbuf[:, ft, s * 128:(s + 1) * 128],
                                                rhs=lnx[:, ft, h * 512:(h + 1) * 512],
                                                start=(ft == 0), stop=(ft == NFT - 1))
                        return inst

                    def mm_up(t, s=s):
                        for h in (0, 1):
                            for ft in range(NFT):
                                inst = t.matmul(out=up_ps[:, h * 512:(h + 1) * 512],
                                                lhsT=ubuf[:, ft, s * 128:(s + 1) * 128],
                                                rhs=lnx[:, ft, h * 512:(h + 1) * 512],
                                                start=(ft == 0), stop=(ft == NFT - 1))
                        return inst
                    og = p.add("tensor", mm_gate, deps=[dg] + ln_dep_all + last_gate_user[0])
                    ou = p.add("tensor", mm_up, deps=[du] + ln_dep_all + last_gate_user[1])
                    panel_readers += [og, ou]
                    o1 = p.add("scalar", lambda s_, og=og: s_.activation(
                        out=t_s2, in_=gate_ps, func=AF.Square), deps=[og])
                    o2 = p.add("vector", lambda v, og=og: v.scalar_tensor_tensor(
                        out=t_a, in0=t_s2, scalar=GELU_ADD, in1=gate_ps,
                        op0=ALU.add, op1=ALU.mult), deps=[o1, og])
                    o3 = p.add("scalar", lambda s_: s_.activation(
                        out=t_sgl, in_=t_a, func=AF.Sigmoid, scale=GELU_SCALE), deps=[o2])
                    o4 = p.add("vector", lambda v, og=og: v.tensor_tensor(
                        out=t_p, in0=t_sgl, in1=gate_ps, op=ALU.mult), deps=[o3, og])
                    o5 = p.add("vector", lambda v, s=s, ou=ou: v.tensor_tensor(
                        out=ibuf[:, s], in0=t_p, in1=up_ps, op=ALU.mult), deps=[o4, ou])
                    inter_ready.append(o5)
                    last_gate_user[0] = [o4]
                    last_gate_user[1] = [o5]
                for ht in range(NFT):
                    for h in (0, 1):
                        slot = dslots[dsl % 2]
                        mywar = dslot_war[dsl % 2]

                        def mm_down(t, ht=ht, h=h, slot=slot):
                            for s in range(NSUB):
                                inst = t.matmul(out=slot,
                                                lhsT=dbuf[:, s, ht * 128:(ht + 1) * 128],
                                                rhs=ibuf[:, s, h * 512:(h + 1) * 512],
                                                start=(s == 0), stop=(s == NSUB - 1))
                            return inst
                        om = p.add("tensor", mm_down, deps=[dd] + inter_ready + mywar)
                        panel_readers.append(om)
                        oe = p.add("vector", lambda v, ht=ht, h=h, slot=slot: v.tensor_tensor(
                            out=xt[:, ht, h * 512:(h + 1) * 512],
                            in0=xt[:, ht, h * 512:(h + 1) * 512],
                            in1=slot, op=ALU.add),
                            deps=[om] + x_accum[ht])
                        x_accum[ht] = [oe] if h == 1 else x_accum[ht] + [oe]
                        dslot_war[dsl % 2] = [oe]
                        dsl += 1
                prev_panel_readers = panel_readers
            xdep = {ft: x_accum[ft] for ft in range(NFT)}
            psum_readers = dslot_war[0] + dslot_war[1] + \
                last_gate_user[0] + last_gate_user[1]
            return xdep, psum_readers

        def dump_x(tag, xd):
            if not DEBUG:
                return
            t = DBG[tag]
            p.add(STAGE_DMA_Q, lambda e, t=t: e.dma_start(
                out=t.rearrange("(ft p) t -> p ft t", p=128), in_=B["xt"]),
                deps=sum((list(v) for v in xd.values()), []), dma=True)

        def finish(extra_deps, xd):
            p.add("gpsimd", lambda e: e.dma_start(
                out=x_out.rearrange("(ft p) t -> p ft t", p=128), in_=B["xt"]),
                deps=sum((list(v) for v in xd.values()), []) + extra_deps, dma=True)
            p.build(stack)

        if UPTO == "A":
            finish([o_ag_g, o_ag_u, o_ag_d], x_dep)
            return nc

        xdep, psr = emit_ffn0(x_dep, [])
        dump_x("dbg_f0a", xdep)
        if UPTO == "B":
            finish([o_ag_g, o_ag_u, o_ag_d], xdep)
            return nc

        p.barrier()
        SC.reset()
        wld2 = SC.tile("wld2", [128, 16, 256], F32)
        wcb2 = SC.tile("wcb2", [128, 16, 256], BF16)
        stg2t = SC.tile("stg2t", [128, 2, 2, 1024], BF16)
        TB = (wld2, wcb2, stg2t)
        prev = []
        w1_sts = []
        for rc in range(2):
            w1_sts += transpose_cast(w1_s[rc * 2048:(rc + 1) * 2048, :], 2048, 256,
                                     w1T_s[:, rc * 2048:(rc + 1) * 2048], after=prev,
                                     bufs=TB)
            prev = w1_sts[-1:]
        w2_sts = []
        for cc in range(2):
            for rc in range(2):
                w2_sts += transpose_cast(
                    w2_s[rc * 2048:(rc + 1) * 2048, cc * 256:(cc + 1) * 256], 2048, 256,
                    w2T_s[cc * 256:(cc + 1) * 256, rc * 2048:(rc + 1) * 2048],
                    after=prev, bufs=TB)
                prev = w2_sts[-1:]
        w3_sts = []
        for cc in range(2):
            w3_sts += transpose_cast(w3_s[:, cc * 256:(cc + 1) * 256], 2048, 256,
                                     w3T_s[cc * 256:(cc + 1) * 256, :], scale=0.5,
                                     after=prev, bufs=TB)
            prev = w3_sts[-1:]
        qkv_sts = []
        for rc in range(3):
            qkv_sts += transpose_cast(qkv_s[rc * 2048:(rc + 1) * 2048, :], 2048, 256,
                                      qkvT_s[:, rc * 2048:(rc + 1) * 2048], after=prev,
                                      bufs=TB)
            prev = qkv_sts[-1:]
        proj_sts = transpose_cast(proj_s, 2048, 256, projT_s, after=prev, bufs=TB)

        SC.reset(32 * 1024)
        wpan = SC.tile("wpan", [128, NFT, 256], F32)
        wpanb = SC.tile("wpanb", [128, NFT, 256], BF16)
        brow_f = SC.tile("brow_f", [1, D], F32)
        brow_b = SC.tile("brow_b", [1, D], BF16)
        stg = SC.tile("stg", [128, 2, TC], BF16)
        onesrow_b = B["onesrow_b"]
        lnx = B["lnx"]

        ln_ops, psa_readers = emit_ln(p, B, xdep, extra_rdeps=consts, psum_war=psr)
        psum_war = {0: psa_readers, 1: psa_readers}
        wpan_war = []
        out_dma = []
        kv_done = {"k0d": [], "v0d": []}
        q_done = []
        for wi, (w, bsrc, out_dram, key) in enumerate(
                ((wq, bq, q0d, "q"), (wk, bk, k0d, "k0d"), (wv, bv, v0d, "v0d"))):
            d_b = p.add("sync", lambda e, bsrc=bsrc: e.dma_start(out=brow_f, in_=bsrc),
                        dma=True, deps=[wpan_war[0]] if wpan_war else [])
            o_bc = p.add("vector", lambda v: v.tensor_copy(out=brow_b, in_=brow_f),
                         deps=[d_b], se=True)
            for pan in range(8):
                dw = p.add("sync", lambda e, w=w, pan=pan: e.dma_start(
                    out=wpan, in_=w.rearrange("(ft q) m -> q ft m", q=128)
                    [:, :, pan * 256:(pan + 1) * 256]), dma=True, deps=wpan_war)
                cw = p.add("vector", lambda v: v.tensor_copy(out=wpanb, in_=wpan), deps=[dw])
                for mt in range(2):
                    m = pan * 2 + mt
                    sel = m % 2
                    mm_ps = ps_a[:, sel * TC:(sel + 1) * TC]

                    def mm(t, mt=mt, m=m, mm_ps=mm_ps):
                        for h in (0, 1):
                            for ft in range(NFT):
                                t.matmul(out=mm_ps[:, h * 512:(h + 1) * 512],
                                         lhsT=wpanb[:, ft, mt * 128:(mt + 1) * 128],
                                         rhs=lnx[:, ft, h * 512:(h + 1) * 512],
                                         start=(ft == 0), stop=False)
                            inst = t.matmul(out=mm_ps[:, h * 512:(h + 1) * 512],
                                            lhsT=brow_b[:, m * 128:(m + 1) * 128],
                                            rhs=onesrow_b[:, h * 512:(h + 1) * 512],
                                            start=False, stop=True)
                        return inst
                    om = p.add("tensor", mm, deps=[cw, o_bc] + ln_ops + psum_war[sel])
                    oe = p.add("vector", lambda v, sel=sel, mm_ps=mm_ps: v.tensor_copy(
                        out=stg[:, sel], in_=mm_ps), deps=[om] + out_dma[-2:])
                    od = p.add("gpsimd", lambda e, m=m, sel=sel, out_dram=out_dram:
                               e.dma_start(out=out_dram[m * 128:(m + 1) * 128, :],
                                           in_=stg[:, sel]), deps=[oe], dma=True)
                    psum_war[sel] = [oe]
                    out_dma.append(od)
                    if key == "q":
                        q_done.append(od)
                    else:
                        kv_done[key].append(od)
                wpan_war = [om]
        qkv0_ps_readers = psum_war[0] + psum_war[1]

        o_ag_k0 = ag(k0d, k0g, RG_PAIR, kv_done["k0d"])
        o_ag_v0 = ag(v0d, v0g, RG_PAIR, kv_done["v0d"])
        if UPTO == "C":
            finish([o_ag_k0, o_ag_v0], xdep)
            return nc

        p.barrier()
        SC.reset(32 * 1024)
        q_sb = SC.tile("q_sb", [128, TC], BF16)
        k_sb = SC.tile("k_sb", [128, 2048], BF16)
        vf_sb = SC.tile("vf_sb", [128, 2048], BF16)
        v_sb = SC.tile("v_sb", [128, 16, 128], BF16)
        p_sb = SC.tile("p_sb", [128, 2, 2048], BF16)
        pt_sb = SC.tile("pt_sb", [128, 2, 128], BF16)
        rmax = SC.tile("rmax", [128, 2], F32)
        negm = SC.tile("negm", [128, 2], F32)
        rsum = SC.tile("rsum", [128, 2], F32)
        rcp = SC.tile("rcp", [128, 2], F32)
        ctx_sb = B["lnx"]
        ps_sc = ps_a
        ps_ctx = ps_b[:, 0:128]

        sc_war = {0: qkv0_ps_readers, 1: qkv0_ps_readers}
        tr_war = [None, None]
        ppwar = [None, None]
        ctx_war = list(ln_ops)
        kv_war = []
        ctx_ops_attn = []
        for hd in range(ATTN_NH):
            d_q = p.add("sync", lambda e, hd=hd: e.dma_start(
                out=q_sb, in_=q0d[hd * 128:(hd + 1) * 128, :]), dma=True,
                deps=q_done[-1:] + kv_war)
            d_k0 = p.add("sync", lambda e, hd=hd: e.dma_start(
                out=k_sb[:, 0:1024], in_=k0g[hd * 128:(hd + 1) * 128, :]), dma=True,
                deps=[o_ag_k0] + kv_war)
            d_k1 = p.add("sync", lambda e, hd=hd: e.dma_start(
                out=k_sb[:, 1024:2048], in_=k0g[2048 + hd * 128:2048 + (hd + 1) * 128, :]),
                dma=True, deps=[o_ag_k0] + kv_war)
            d_v0 = p.add("sync", lambda e, hd=hd: e.dma_start(
                out=vf_sb[:, 0:1024], in_=v0g[hd * 128:(hd + 1) * 128, :]), dma=True,
                deps=[o_ag_v0] + kv_war)
            d_v1 = p.add("sync", lambda e, hd=hd: e.dma_start(
                out=vf_sb[:, 1024:2048], in_=v0g[2048 + hd * 128:2048 + (hd + 1) * 128, :]),
                dma=True, deps=[o_ag_v0] + kv_war)
            v_evs = []
            if ATTN_PARTS == "dma":
                kv_war = [d_q, d_k0, d_k1, d_v0, d_v1]
                head_ce = [d_v1]
                ctx_ops_attn.append(d_v1)
                continue
            for kt in range(16):
                sel = kt % 2
                o_tr = p.add("tensor", lambda t, kt=kt, sel=sel: t.transpose(
                    out=ps_t[:, sel * 1024:sel * 1024 + 128],
                    in_=vf_sb[:, kt * 128:(kt + 1) * 128], identity=identt),
                    deps=[d_v0, d_v1, d_id, tr_war[sel]])
                o_ve = p.add("vector", lambda v, kt=kt, sel=sel: v.tensor_copy(
                    out=v_sb[:, kt, :], in_=ps_t[:, sel * 1024:sel * 1024 + 128]),
                    deps=[o_tr] + v_evs[-2:-1])
                tr_war[sel] = o_ve
                v_evs.append(o_ve)
            kv_war = []
            head_ce = []
            if ATTN_PARTS == "vtr":
                kv_war = v_evs[-1:]
                ctx_ops_attn.append(v_evs[-1])
                continue
            for qt in range(8):
                r = qt % 2
                nk = 1024 + (qt + 1) * 128
                pp = p_sb[:, r]

                def mm_sc(t, qt=qt, nk=nk):
                    for kc in range(0, nk, 512):
                        w = min(512, nk - kc)
                        inst = t.matmul(out=ps_sc[:, kc:kc + w],
                                        lhsT=q_sb[:, qt * 128:(qt + 1) * 128],
                                        rhs=k_sb[:, kc:kc + w], start=True, stop=True)
                    return inst
                o_sc = p.add("tensor", mm_sc, deps=[d_q, d_k0, d_k1] + sc_war[0] + sc_war[1])
                if ATTN_PARTS == "score":
                    sc_war[0] = [o_sc]
                    sc_war[1] = [o_sc]
                    head_ce.append(o_sc)
                    ctx_ops_attn.append(o_sc)
                    continue
                o_max = p.add("vector", lambda v, nk=nk, r=r: v.reduce_max(
                    out=rmax[:, r:r + 1], in_=ps_sc[:, 0:nk], axis=AX.X), deps=[o_sc],
                    se=True)
                o_ng = p.add("vector", lambda v, r=r: v.tensor_single_scalar(
                    out=negm[:, r:r + 1], in_=rmax[:, r:r + 1], scalar=-ATT_SC,
                    op=ALU.mult), deps=[o_max], se=True)
                o_exp = p.add("scalar", lambda s, nk=nk, r=r, pp=pp: s.activation(
                    out=pp[0:128, 0:nk], in_=ps_sc[:, 0:nk], func=AF.Exp,
                    scale=ATT_SC, bias=negm[:, r:r + 1]),
                    deps=[o_ng, o_sc, ppwar[r]], se=True)
                sc_war[0] = [o_exp]
                sc_war[1] = [o_exp]
                o_mA = p.add("vector", lambda v, qt=qt, pp=pp: v.tensor_tensor(
                    out=pp[0:128, qt * 128:(qt + 1) * 128],
                    in0=pp[0:128, qt * 128:(qt + 1) * 128], in1=maskA, op=ALU.mult),
                    deps=[o_exp, d_mA], se=True)
                o_mC = p.add("vector", lambda v, qt=qt, pp=pp: v.tensor_tensor(
                    out=pp[0:128, (qt + 1) * 128:(qt + 1) * 128 + 1024],
                    in0=pp[0:128, (qt + 1) * 128:(qt + 1) * 128 + 1024], in1=maskC,
                    op=ALU.mult), deps=[o_exp, d_mC], se=True)
                o_mB = p.add("vector", lambda v, qt=qt, pp=pp: v.tensor_tensor(
                    out=pp[0:128, 1024 + qt * 128:1024 + (qt + 1) * 128],
                    in0=pp[0:128, 1024 + qt * 128:1024 + (qt + 1) * 128], in1=maskB,
                    op=ALU.mult), deps=[o_mC, d_mB], se=True)
                o_sum = p.add("vector", lambda v, nk=nk, r=r, pp=pp: v.reduce_sum(
                    out=rsum[:, r:r + 1], in_=pp[0:128, 0:nk], axis=AX.X),
                    deps=[o_mA, o_mB], se=True)
                o_rcp = p.add("vector", lambda v, r=r: v.reciprocal(
                    out=rcp[:, r:r + 1], in_=rsum[:, r:r + 1]), deps=[o_sum], se=True)
                o_nrm = p.add("vector", lambda v, nk=nk, r=r, pp=pp: v.tensor_single_scalar(
                    out=pp[0:128, 0:nk], in_=pp[0:128, 0:nk], scalar=rcp[:, r:r + 1],
                    op=ALU.mult), deps=[o_rcp], se=True)
                if ATTN_PARTS == "soft":
                    ppwar[r] = o_nrm
                    sc_war[0] = [o_nrm]
                    sc_war[1] = [o_nrm]
                    head_ce.append(o_nrm)
                    ctx_ops_attn.append(o_nrm)
                    continue
                o_cm = None
                nkt = 8 + qt + 1
                for kt in range(nkt):
                    sel = kt % 2
                    o_tr = p.add("tensor", lambda t, kt=kt, sel=sel, pp=pp: t.transpose(
                        out=ps_t[:, sel * 1024:sel * 1024 + 128],
                        in_=pp[0:128, kt * 128:(kt + 1) * 128], identity=identt),
                        deps=[o_nrm, d_id, tr_war[sel]])
                    o_pe = p.add("vector", lambda v, sel=sel: v.tensor_copy(
                        out=pt_sb[:, sel], in_=ps_t[:, sel * 1024:sel * 1024 + 128]),
                        deps=[o_tr])
                    cdeps = [o_pe] + v_evs
                    if kt == 0:
                        cdeps += ctx_war
                    o_cm = p.add("tensor", lambda t, kt=kt, sel=sel, nkt=nkt: t.matmul(
                        out=ps_ctx, lhsT=v_sb[:, kt, :], rhs=pt_sb[:, sel],
                        start=(kt == 0), stop=(kt == nkt - 1)), deps=cdeps)
                    tr_war[sel] = o_cm
                    if kt == nkt - 1:
                        ppwar[r] = o_tr
                o_ce = p.add("vector", lambda v, hd=hd, qt=qt: v.tensor_copy(
                    out=ctx_sb[:, hd, qt * 128:(qt + 1) * 128], in_=ps_ctx), deps=[o_cm])
                ctx_war = [o_ce]
                head_ce.append(o_ce)
                ctx_ops_attn.append(o_ce)
            kv_war = head_ce[-1:]

        if DEBUG:
            p.add(STAGE_DMA_Q, lambda e: e.dma_start(
                out=DBG["dbg_ctx"].rearrange("(ft p) t -> p ft t", p=128),
                in_=B["lnx"]), deps=list(ctx_ops_attn), dma=True)
        o_ag_w1 = ag(w1T_s, w1T, RG_ALL, w1_sts)
        o_ag_w2 = ag(w2T_s, w2T, RG_ALL, w2_sts)
        o_ag_w3 = ag(w3T_s, w3T, RG_ALL, w3_sts)
        o_ag_qkv = ag(qkvT_s, qkvT, RG_ALL, qkv_sts)
        o_ag_proj = ag(projT_s, projT, RG_ALL, proj_sts)
        if UPTO == "D":
            finish(ctx_ops_attn[-1:], xdep)
            return nc

        p.barrier()
        SC.reset(32 * 1024)
        wpanP = SC.tile("wpan0", [128, NFT, 256], F32)
        wpanbP = SC.tile("wpanb0", [128, NFT, 256], BF16)
        brow_fP = SC.tile("brow_f0", [1, D], F32)
        brow_bP = SC.tile("brow_b0", [1, D], BF16)
        xt = B["xt"]
        d_bo = p.add("sync", lambda e: e.dma_start(out=brow_fP, in_=bo), dma=True)
        o_bc = p.add("vector", lambda v: v.tensor_copy(out=brow_bP, in_=brow_fP),
                     deps=[d_bo], se=True)
        psum_war = {0: sc_war[0], 1: sc_war[1]}
        wpan_war = []
        x_done = {}
        for pan in range(8):
            dw = p.add("sync", lambda e, pan=pan: e.dma_start(
                out=wpanP, in_=wo.rearrange("(ft q) m -> q ft m", q=128)
                [:, :, pan * 256:(pan + 1) * 256]), dma=True, deps=wpan_war)
            cw = p.add("vector", lambda v: v.tensor_copy(out=wpanbP, in_=wpanP), deps=[dw])
            for mt in range(2):
                m = pan * 2 + mt
                sel = m % 2
                mm_ps = ps_a[:, sel * TC:(sel + 1) * TC]

                def mm(t, mt=mt, m=m, mm_ps=mm_ps):
                    for h in (0, 1):
                        for ft in range(NFT):
                            t.matmul(out=mm_ps[:, h * 512:(h + 1) * 512],
                                     lhsT=wpanbP[:, ft, mt * 128:(mt + 1) * 128],
                                     rhs=ctx_sb[:, ft, h * 512:(h + 1) * 512],
                                     start=(ft == 0), stop=False)
                        inst = t.matmul(out=mm_ps[:, h * 512:(h + 1) * 512],
                                        lhsT=brow_bP[:, m * 128:(m + 1) * 128],
                                        rhs=onesrow_b[:, h * 512:(h + 1) * 512],
                                        start=False, stop=True)
                    return inst
                om = p.add("tensor", mm,
                           deps=[cw, o_bc] + ctx_ops_attn[-1:] + consts + psum_war[sel])
                oa = p.add("vector", lambda v, m=m, mm_ps=mm_ps: v.tensor_tensor(
                    out=xt[:, m], in0=xt[:, m], in1=mm_ps, op=ALU.add),
                    deps=[om] + xdep[m])
                psum_war[sel] = [oa]
                x_done[m] = [oa]
            wpan_war = [om]

        dump_x("dbg_p0", x_done)

        p.barrier()
        xdep, psr = emit_ffn0(x_done, psum_war[0] + psum_war[1])
        dump_x("dbg_f0b", xdep)
        if UPTO == "E":
            finish([], xdep)
            return nc

        def emit_ffn1(x_dep, prior_psum_readers, store_out=False):
            SC.reset()
            h1 = SC.tile("h1", [128, 32, 512], BF16)
            wpan = SC.tile("wpanf1", [128, 32, 2, 128], BF16)
            btmp_f = SC.tile("btmp_f", [1, 512], F32)
            b12_b = SC.tile("b12_b", [1, 2, 4096], BF16)
            b3row_b = SC.tile("b3row_b", [1, D], BF16)
            lnx = B["lnx"]
            xt = B["xt"]
            h2h = lnx.rearrange("p a t -> p (a t)").rearrange("p (a t) -> p a t", a=32)
            onesrow_b = B["onesrow_b"]

            w1T_v = w1T.rearrange("(ft q) m -> q ft m", q=128)
            w2T_v = w2T.rearrange("(kt q) m -> q kt m", q=128)
            w3T_v = w3T.rearrange("(kt q) m -> q kt m", q=128)
            wpan_war = [None, None]
            x_done = {ft: list(x_dep[ft]) for ft in range(NFT)}
            out_sts = []
            psr = list(prior_psum_readers)
            o_b1 = o_b2 = o_b3 = None
            for hh in (0, 1):
                p.barrier()
                ln_ops, psa_readers = emit_ln(p, B, x_done, extra_rdeps=consts,
                                              psum_war=psr)
                if hh == 0:
                    o_b1 = emit_bias_cast(p, btmp_f, b12_b[:, 0], b1, 4096)
                    o_b2 = emit_bias_cast(p, btmp_f, b12_b[:, 1], b2, 4096, after=[o_b1])
                    o_b3 = emit_bias_cast(p, btmp_f, b3row_b, b3, D, scale=0.5,
                                          after=[o_b2])
                psum_war = {i: psa_readers for i in range(4)}
                h1_ready = []
                for m in range(32):
                    sel = m % 2
                    dw = p.add("sync", lambda e, m=m, sel=sel: e.dma_start(
                        out=wpan[:, 0:NFT, sel, :], in_=w1T_v[:, :, m * 128:(m + 1) * 128]),
                        dma=True, deps=[wpan_war[sel], o_ag_w1])
                    mm_ps = ps_a[:, sel * 512:(sel + 1) * 512]

                    def mm(t, m=m, sel=sel, hh=hh, mm_ps=mm_ps):
                        for ft in range(NFT):
                            t.matmul(out=mm_ps, lhsT=wpan[:, ft, sel, :],
                                     rhs=lnx[:, ft, hh * 512:(hh + 1) * 512],
                                     start=(ft == 0), stop=False)
                        return t.matmul(out=mm_ps, lhsT=b12_b[:, 0, m * 128:(m + 1) * 128],
                                        rhs=onesrow_b[:, 0:512], start=False, stop=True)
                    om = p.add("tensor", mm, deps=[dw, o_b1] + ln_ops + psum_war[sel])
                    oe = p.add("scalar", lambda s, m=m, mm_ps=mm_ps: s.activation(
                        out=h1[:, m], in_=mm_ps, func=AF.Relu), deps=[om])
                    psum_war[sel] = [oe]
                    wpan_war[sel] = om
                    h1_ready.append(oe)
                h2_ready = []
                h2_war = list(ln_ops)
                for m in range(32):
                    sel = 2 + m % 2
                    dw = p.add("sync", lambda e, m=m, sel=sel: e.dma_start(
                        out=wpan[:, :, sel - 2, :], in_=w2T_v[:, :, m * 128:(m + 1) * 128]),
                        dma=True, deps=[wpan_war[sel - 2], o_ag_w2])
                    mm_ps = ps_a[:, sel * 512:(sel + 1) * 512]

                    def mm(t, m=m, sel=sel, mm_ps=mm_ps):
                        for kt in range(32):
                            t.matmul(out=mm_ps, lhsT=wpan[:, kt, sel - 2, :],
                                     rhs=h1[:, kt, :], start=(kt == 0), stop=False)
                        return t.matmul(out=mm_ps, lhsT=b12_b[:, 1, m * 128:(m + 1) * 128],
                                        rhs=onesrow_b[:, 0:512], start=False, stop=True)
                    om = p.add("tensor", mm, deps=[dw, o_b2] + h1_ready + psum_war[sel])
                    oe = p.add("scalar", lambda s, m=m, mm_ps=mm_ps: s.activation(
                        out=h2h[:, m, 0:512], in_=mm_ps, func=AF.Relu),
                        deps=[om] + h2_war)
                    psum_war[sel] = [oe]
                    wpan_war[sel - 2] = om
                    h2_ready.append(oe)
                w3_psum_war = [None, None]
                for fo in range(NFT):
                    sel = fo % 2
                    dw = p.add("sync", lambda e, fo=fo, sel=sel: e.dma_start(
                        out=wpan[:, :, sel, :], in_=w3T_v[:, :, fo * 128:(fo + 1) * 128]),
                        dma=True, deps=[wpan_war[sel], o_ag_w3])
                    mm_ps = ps_b[:, sel * 512:(sel + 1) * 512]

                    def mm(t, fo=fo, sel=sel, mm_ps=mm_ps):
                        for kt in range(32):
                            t.matmul(out=mm_ps, lhsT=wpan[:, kt, sel, :],
                                     rhs=h2h[:, kt, 0:512], start=(kt == 0), stop=False)
                        return t.matmul(out=mm_ps,
                                        lhsT=b3row_b[:, fo * 128:(fo + 1) * 128],
                                        rhs=onesrow_b[:, 0:512], start=False, stop=True)
                    om = p.add("tensor", mm,
                               deps=[dw, o_b3, w3_psum_war[sel]] + h2_ready)
                    oa = p.add("vector", lambda v, fo=fo, hh=hh, sel=sel, mm_ps=mm_ps:
                               v.tensor_tensor(
                        out=xt[:, fo, hh * 512:(hh + 1) * 512],
                        in0=xt[:, fo, hh * 512:(hh + 1) * 512],
                        in1=mm_ps, op=ALU.add), deps=[om] + x_done[fo])
                    x_done[fo] = [oa]
                    w3_psum_war[sel] = oa
                    wpan_war[sel] = om
                    if store_out:
                        st = p.add("gpsimd", lambda e, fo=fo, hh=hh: e.dma_start(
                            out=x_out.rearrange("(ft p) t -> p ft t", p=128)
                            [:, fo, hh * 512:(hh + 1) * 512],
                            in_=xt[:, fo, hh * 512:(hh + 1) * 512]), deps=[oa], dma=True)
                        out_sts.append(st)
                psr = sum((psum_war[i] for i in range(4)), []) + \
                    [w3_psum_war[0], w3_psum_war[1]]
            return x_done, psr, out_sts

        p.barrier()
        xdep, psr, _ = emit_ffn1(xdep, psr)
        dump_x("dbg_f1a", xdep)
        if UPTO == "F":
            finish([], xdep)
            return nc

        p.barrier()
        SC.reset()
        wpanb1 = SC.tile("wpanb1", [128, NFT, 512], BF16)
        stg1 = SC.tile("stg1", [128, 2, TC], BF16)
        lnx = B["lnx"]
        ln_ops, psa_readers = emit_ln(p, B, xdep, extra_rdeps=consts, psum_war=psr)
        qkvT_v = qkvT.rearrange("(ft q) m -> q ft m", q=128)
        psum_war = {0: psa_readers, 1: psa_readers}
        wpan_war = []
        out_dma = []
        q1_done = []
        kv1_done = []
        for pan in range(4):
            dw = p.add("sync", lambda e, pan=pan: e.dma_start(
                out=wpanb1, in_=qkvT_v[:, :, pan * 512:(pan + 1) * 512]), dma=True,
                deps=wpan_war + [o_ag_qkv])
            for mt in range(4):
                m = pan * 4 + mt
                sel = m % 2
                mm_ps = ps_a[:, sel * TC:(sel + 1) * TC]

                def mm(t, mt=mt, mm_ps=mm_ps):
                    for h in (0, 1):
                        for ft in range(NFT):
                            inst = t.matmul(out=mm_ps[:, h * 512:(h + 1) * 512],
                                            lhsT=wpanb1[:, ft, mt * 128:(mt + 1) * 128],
                                            rhs=lnx[:, ft, h * 512:(h + 1) * 512],
                                            start=(ft == 0), stop=(ft == NFT - 1))
                    return inst
                om = p.add("tensor", mm, deps=[dw] + ln_ops + psum_war[sel])
                oe = p.add("scalar", lambda s, sel=sel, mm_ps=mm_ps: s.activation(
                    out=stg1[:, sel], in_=mm_ps, func=AF.Relu), deps=[om] + out_dma[-2:])
                od = p.add("gpsimd", lambda e, m=m, sel=sel: e.dma_start(
                    out=q1d[m * 128:(m + 1) * 128, :], in_=stg1[:, sel]), deps=[oe],
                    dma=True)
                psum_war[sel] = [oe]
                out_dma.append(od)
                q1_done.append(od)
            wpan_war = [om]
        for wi, (coff, out_dram, func) in enumerate(((2048, k1d, AF.Relu),
                                                     (4096, v1d, None))):
            for ch in range(4):
                dw = p.add("sync", lambda e, ch=ch, coff=coff: e.dma_start(
                    out=wpanb1, in_=qkvT_v[:, :, coff + ch * 512:coff + (ch + 1) * 512]),
                    dma=True, deps=wpan_war)
                for tt in range(8):
                    sel = tt % 2
                    mm_ps = ps_a[:, sel * TC:sel * TC + 512]

                    def mm(t, ch=ch, tt=tt, mm_ps=mm_ps):
                        for ft in range(NFT):
                            inst = t.matmul(out=mm_ps,
                                            lhsT=lnx[:, ft, tt * 128:(tt + 1) * 128],
                                            rhs=wpanb1[:, ft, :],
                                            start=(ft == 0), stop=(ft == NFT - 1))
                        return inst
                    om = p.add("tensor", mm, deps=[dw] + ln_ops + psum_war[sel])
                    if func is None:
                        oe = p.add("vector", lambda v, sel=sel, mm_ps=mm_ps: v.tensor_copy(
                            out=stg1[:, sel, 0:512], in_=mm_ps), deps=[om] + out_dma[-2:])
                    else:
                        oe = p.add("scalar", lambda s, sel=sel, mm_ps=mm_ps: s.activation(
                            out=stg1[:, sel, 0:512], in_=mm_ps, func=AF.Relu),
                            deps=[om] + out_dma[-2:])
                    od = p.add("gpsimd", lambda e, tt=tt, ch=ch, sel=sel,
                               out_dram=out_dram: e.dma_start(
                        out=out_dram[tt * 128:(tt + 1) * 128, ch * 512:(ch + 1) * 512],
                        in_=stg1[:, sel, 0:512]), deps=[oe], dma=True)
                    psum_war[sel] = [oe]
                    out_dma.append(od)
                    kv1_done.append(od)
                wpan_war = [om]
        qkv1_ps_readers = psum_war[0] + psum_war[1]

        p.barrier()
        SC.reset()
        kt_sb = SC.tile("kt_sb", [128, 8, 2048], BF16)
        vt_sb = SC.tile("vt_sb", [128, 8, 2048], BF16)
        vk_sb = SC.tile("vk_sb", [32, 2112], F32)
        d_k = p.add("sync", lambda e: e.dma_start(
            out=kt_sb, in_=k1d.rearrange("(tt q) f -> q tt f", q=128)), dma=True,
            deps=kv1_done[-1:])
        d_v = p.add("sync", lambda e: e.dma_start(
            out=vt_sb, in_=v1d.rearrange("(tt q) f -> q tt f", q=128)), dma=True,
            deps=kv1_done[-1:])
        onescol = B["onescol"]
        ps_war = [qkv1_ps_readers, qkv1_ps_readers]
        evs = []
        for hd in range(64):
            sel = hd % 2
            slot = ps_b[0:32, sel * 512:sel * 512 + 33]

            def mm(t, hd=hd, slot=slot):
                for tt in range(8):
                    t.matmul(out=slot[:, 0:32], lhsT=kt_sb[:, tt, hd * 32:(hd + 1) * 32],
                             rhs=vt_sb[:, tt, hd * 32:(hd + 1) * 32],
                             start=(tt == 0), stop=(tt == 7))
                for tt in range(8):
                    inst = t.matmul(out=slot[:, 32:33],
                                    lhsT=kt_sb[:, tt, hd * 32:(hd + 1) * 32],
                                    rhs=onescol, start=(tt == 0), stop=(tt == 7))
                return inst
            om = p.add("tensor", mm, deps=[d_k, d_v] + consts + ps_war[sel])
            oe = p.add("vector", lambda v, hd=hd, slot=slot: v.tensor_copy(
                out=vk_sb[:, hd * 33:(hd + 1) * 33], in_=slot), deps=[om])
            ps_war[sel] = [oe]
            evs.append(oe)
        d_vk = p.add("sync", lambda e: e.dma_start(out=vkp, in_=vk_sb), deps=evs[-1:] + evs[-2:-1], dma=True)
        o_ar_vk = p.add("gpsimd", lambda g: g.collective_compute(
            "AllReduce", ALU.add, replica_groups=RG_PAIR,
            ins=[vkp.opt()], outs=[vkr.opt()]), deps=[d_vk], cc=True)
        if UPTO == "G":
            finish([o_ar_vk], xdep)
            return nc

        p.barrier()
        SC.reset()
        q_sb1 = SC.tile("q_sb1", [128, NFT, TC], BF16)
        bd_f = SC.tile("bd_f", [128, NFT, 128], F32)
        bd_b = SC.tile("bd_b", [128, NFT, 128], BF16)
        den_f = SC.tile("den_f", [128, NFT, 4], F32)
        den_b = SC.tile("den_b", [128, NFT, 4], BF16)
        den_sb = SC.tile("den_sb", [4, TC], F32)
        rcp_sb = SC.tile("rcp_sb", [4, TC], F32)
        rcp_bf = SC.tile("rcp_bf", [4, TC], BF16)
        num_sb = SC.tile("num_sb", [128, TC], BF16)
        wpanb2 = SC.tile("wpanb2", [128, NFT, 256], BF16)
        btmp_f2 = SC.tile("btmp_f2", [1, 512], F32)
        brow_b2 = SC.tile("brow_b2", [1, D], BF16)
        ctx_sb = B["lnx"]
        xt = B["xt"]
        onesrow_b = B["onesrow_b"]

        d_q1 = p.add("sync", lambda e: e.dma_start(
            out=q_sb1, in_=q1d.rearrange("(ft q) t -> q ft t", q=128)), dma=True,
            deps=q1_done[-1:])
        o_z1 = p.add("vector", lambda v: v.memset(bd_f, 0.0), se=True)
        o_z2 = p.add("vector", lambda v: v.memset(den_f, 0.0), se=True)
        bd_dmas = []
        for g in range(16):
            for j in range(4):
                h = g * 4 + j
                bd_dmas.append(p.add("sync", lambda e, g=g, j=j, h=h: e.dma_start(
                    out=bd_f[j * 32:(j + 1) * 32, g, j * 32:(j + 1) * 32],
                    in_=vkr[0:32, h * 33:h * 33 + 32]), dma=True,
                    deps=[o_ar_vk, o_z1]))
                bd_dmas.append(p.add("sync", lambda e, g=g, j=j, h=h: e.dma_start(
                    out=den_f[j * 32:(j + 1) * 32, g, j:j + 1],
                    in_=vkr[0:32, h * 33 + 32:h * 33 + 33],
                    allow_slow_non_contiguous=True), dma=True,
                    deps=[o_ar_vk, o_z2]))
        c_bd = p.add("vector", lambda v: v.tensor_copy(out=bd_b, in_=bd_f), deps=bd_dmas)
        c_dn = p.add("vector", lambda v: v.tensor_copy(out=den_b, in_=den_f), deps=bd_dmas)
        c_bp = emit_bias_cast(p, btmp_f2, brow_b2, bproj, D)

        ps_num = ps_a[:, 0:TC]
        ps_dbc = ps_a[:, TC:2 * TC]
        ps_den = ps_b[0:4, 0:TC]
        num_war = ps_war[0]
        dbc_war = ps_war[1]
        den_war = ps_war[0] + ps_war[1]
        ctx_ops = []
        ln_war = list(ln_ops)
        for g in range(NFT):
            o_nm = p.add("tensor", lambda t, g=g: [
                t.matmul(out=ps_num[:, h * 512:(h + 1) * 512], lhsT=bd_b[:, g, :],
                         rhs=q_sb1[:, g, h * 512:(h + 1) * 512], start=True, stop=True)
                for h in (0, 1)][-1], deps=[c_bd, d_q1] + num_war)
            o_dm = p.add("tensor", lambda t, g=g: [
                t.matmul(out=ps_den[:, h * 512:(h + 1) * 512], lhsT=den_b[:, g, :],
                         rhs=q_sb1[:, g, h * 512:(h + 1) * 512], start=True, stop=True)
                for h in (0, 1)][-1], deps=[c_dn, d_q1] + den_war)
            o_ep = p.add("vector", lambda v: v.tensor_single_scalar(
                out=den_sb, in_=ps_den, scalar=EPS_A1, op=ALU.add), deps=[o_dm], se=True)
            o_rc = p.add("vector", lambda v: v.reciprocal(out=rcp_sb, in_=den_sb),
                         deps=[o_ep], se=True)
            o_rb = p.add("vector", lambda v: v.tensor_copy(out=rcp_bf, in_=rcp_sb),
                         deps=[o_rc], se=True)
            den_war = [o_ep]
            o_db = p.add("tensor", lambda t: [
                t.matmul(out=ps_dbc[:, h * 512:(h + 1) * 512], lhsT=bonest,
                         rhs=rcp_bf[:, h * 512:(h + 1) * 512], start=True, stop=True)
                for h in (0, 1)][-1], deps=[o_rb, d_bn] + dbc_war)
            o_ne = p.add("vector", lambda v: v.tensor_copy(out=num_sb, in_=ps_num),
                         deps=[o_nm])
            num_war = [o_ne]
            o_cx = p.add("vector", lambda v, g=g: v.tensor_tensor(
                out=ctx_sb[:, g], in0=num_sb, in1=ps_dbc, op=ALU.mult),
                deps=[o_ne, o_db] + ln_war)
            ln_war = []
            dbc_war = [o_cx]
            ctx_ops.append(o_cx)
        projT_v = projT.rearrange("(ft q) m -> q ft m", q=128)
        psum_war = {0: num_war + [ctx_ops[-1]], 1: dbc_war + [ctx_ops[-1]]}
        wpan_war = []
        x_done2 = {}
        for pan in range(8):
            dw = p.add("sync", lambda e, pan=pan: e.dma_start(
                out=wpanb2, in_=projT_v[:, :, pan * 256:(pan + 1) * 256]), dma=True,
                deps=wpan_war + [o_ag_proj])
            for mt in range(2):
                m = pan * 2 + mt
                sel = m % 2
                mm_ps = ps_a[:, sel * TC:(sel + 1) * TC]

                def mm(t, mt=mt, m=m, mm_ps=mm_ps):
                    for h in (0, 1):
                        for ft in range(NFT):
                            t.matmul(out=mm_ps[:, h * 512:(h + 1) * 512],
                                     lhsT=wpanb2[:, ft, mt * 128:(mt + 1) * 128],
                                     rhs=ctx_sb[:, ft, h * 512:(h + 1) * 512],
                                     start=(ft == 0), stop=False)
                        inst = t.matmul(out=mm_ps[:, h * 512:(h + 1) * 512],
                                        lhsT=brow_b2[:, m * 128:(m + 1) * 128],
                                        rhs=onesrow_b[:, h * 512:(h + 1) * 512],
                                        start=False, stop=True)
                    return inst
                om = p.add("tensor", mm,
                           deps=[dw, c_bp] + ctx_ops[-1:] + consts + psum_war[sel])
                oa = p.add("vector", lambda v, m=m, mm_ps=mm_ps: v.tensor_tensor(
                    out=xt[:, m], in0=xt[:, m], in1=mm_ps, op=ALU.add),
                    deps=[om] + xdep[m])
                psum_war[sel] = [oa]
                x_done2[m] = [oa]
            wpan_war = [om]

        dump_x("dbg_a1", x_done2)
        if UPTO == "H":
            finish([], x_done2)
            return nc

        p.barrier()
        xdep, psr, out_sts = emit_ffn1(x_done2, psum_war[0] + psum_war[1],
                                       store_out=True)
        p.build(stack)
    return nc



_IDENT = np.eye(128, dtype=BF)
_TRIL = np.tril(np.ones((128, 128), np.float32)).astype(BF)
_ONES128 = np.ones((128, 128), BF)
_BONES4 = np.zeros((4, 128), np.float32)
for _j in range(4):
    _BONES4[_j, _j * 32:(_j + 1) * 32] = 1.0
_BONES4 = _BONES4.astype(BF)

_NC_CACHE = {}


def kernel(**inputs):
    inputs = {k: np.asarray(v) for k, v in inputs.items()}
    x = np.asarray(inputs["L0_attn_x"], dtype=np.float32)
    x_fm = np.ascontiguousarray(x.reshape(8192, 2048).T)
    if "mega" not in _NC_CACHE:
        _NC_CACHE["mega"] = build_mega()
    nc = _NC_CACHE["mega"]

    gp = inputs["L0_ffn_gate_W_packed"]; gs = inputs["L0_ffn_gate_scales"]
    up = inputs["L0_ffn_up_W_packed"]; us = inputs["L0_ffn_up_scales"]
    dp = inputs["L0_ffn_down_W_packed"]; ds = inputs["L0_ffn_down_scales"]
    in_maps = []
    for c in range(NCORE):
        odd = c % 2 == 1
        in_maps.append({
            "x_in": np.ascontiguousarray(x_fm[:, c * TC:(c + 1) * TC]),
            "wq": inputs["L0_attn_Wq"], "wk": inputs["L0_attn_Wk"],
            "wv": inputs["L0_attn_Wv"], "wo": inputs["L0_attn_Wo"],
            "bq": inputs["L0_attn_bq"].reshape(1, D),
            "bk": inputs["L0_attn_bk"].reshape(1, D),
            "bv": inputs["L0_attn_bv"].reshape(1, D),
            "bo": inputs["L0_attn_bo"].reshape(1, D),
            "g_pk": np.ascontiguousarray(gp[:, c * 128:(c + 1) * 128]),
            "g_sc": np.ascontiguousarray(gs[:, c * 8:(c + 1) * 8]),
            "u_pk": np.ascontiguousarray(up[:, c * 128:(c + 1) * 128]),
            "u_sc": np.ascontiguousarray(us[:, c * 8:(c + 1) * 8]),
            "d_pk": np.ascontiguousarray(dp[:, c * 1024:(c + 1) * 1024]),
            "d_sc": np.ascontiguousarray(ds[:, c * 64:(c + 1) * 64]),
            "w1_s": np.ascontiguousarray(inputs["L1_ffn_W1"][:, c * 256:(c + 1) * 256]),
            "w2_s": np.ascontiguousarray(inputs["L1_ffn_W2"][:, c * 512:(c + 1) * 512]),
            "w3_s": np.ascontiguousarray(inputs["L1_ffn_W3"][:, c * 512:(c + 1) * 512]),
            "qkv_s": np.ascontiguousarray(inputs["L1_attn_w_qkv"][:, c * 256:(c + 1) * 256]),
            "proj_s": np.ascontiguousarray(inputs["L1_attn_w_proj"][:, c * 256:(c + 1) * 256]),
            "b1": inputs["L1_ffn_b1"].reshape(1, 4096),
            "b2": inputs["L1_ffn_b2"].reshape(1, 4096),
            "b3": inputs["L1_ffn_b3"].reshape(1, D),
            "bproj": inputs["L1_attn_b_proj"].reshape(1, D),
            "ident": _IDENT,
            "maskA": _ONES128 if odd else _TRIL,
            "maskB": _TRIL if odd else _ONES128,
            "maskC": (np.ones if odd else np.zeros)((128, 1024), np.float32).astype(BF),
            "bones": _BONES4,
        })
    res = run_bass_kernel_spmd(nc, in_maps, core_ids=list(range(NCORE)))
    slabs = [res.results[c]["x_out"] for c in range(NCORE)]
    x_fm_out = np.concatenate(slabs, axis=1)
    return np.ascontiguousarray(x_fm_out.T).reshape(4, 2048, 2048).astype(np.float32)
